# version 05
# baseline (speedup 1.0000x reference)
import numpy as np
import ml_dtypes
from contextlib import ExitStack
import concourse.bass as bass
import concourse.mybir as mybir
from concourse.bass_utils import run_bass_kernel_spmd

F32 = mybir.dt.float32
BF16 = mybir.dt.bfloat16
I32 = mybir.dt.int32
ALU = mybir.AluOpType
AF = mybir.ActivationFunctionType
AX = mybir.AxisListType.X

TOK = 1024
D = 2048
NST = 16
CH = 2080
FF = 8192
NCORES = 8
KT = D // 128
TTL = TOK // 128
FTL = FF // 128
LN_EPS = 1e-5
RMS_EPS = 1e-6
Q_EPS = 1e-5
SEMS = ("LD", "ST", "VV", "SS", "TN", "CC",
        "LX0", "LX1", "LW0", "LW1", "LS", "LT0", "LT1", "LR0", "LR1",
        "SC", "SP", "SX0", "SX1", "SY0", "SY1", "SW0", "SW1", "SO0", "SO1")


class Prog:
    ENGS = ("sync", "vector", "scalar", "tensor", "gpsimd")

    def __init__(self):
        self.q = {e: [] for e in self.ENGS}
        self.sems = {}

    def reg(self, name, handle):
        self.sems[name] = [handle, 0]

    def emit(self, eng, f):
        self.q[eng].append(f)

    def val(self, name):
        return self.sems[name][1]

    def dma(self, eng, out, in_, sem="LD", **kw):
        h = self.sems[sem]
        h[1] += 16
        self.emit(eng, lambda E, o=out, i=in_, s=h[0], kw=kw: E.dma_start(out=o, in_=i, **kw).then_inc(s, 16))
        return h[1]

    def dmaT(self, eng, out, in_, sem="LD"):
        h = self.sems[sem]
        h[1] += 16
        self.emit(eng, lambda E, o=out, i=in_, s=h[0]: E.dma_start_transpose(o, i).then_inc(s, 16))
        return h[1]

    def wait(self, eng, *names):
        pairs = tuple((self.sems[n][0], self.sems[n][1]) for n in names if self.sems[n][1] > 0)

        def f(E, ps=pairs):
            for hh, v in ps:
                E.wait_ge(hh, v)
        self.emit(eng, f)

    def wait_at(self, eng, sem, v):
        if v <= 0:
            return
        h = self.sems[sem][0]
        self.emit(eng, lambda E, hh=h, vv=v: E.wait_ge(hh, vv))

    def op(self, eng, fn, sem=None):
        if sem is None:
            self.emit(eng, fn)
            return None
        h = self.sems[sem]
        h[1] += 1
        self.emit(eng, lambda E, f=fn, s=h[0]: f(E).then_inc(s, 1))
        return h[1]

    def barrier(self, *engs):
        for e in engs:
            self.wait(e, *SEMS)


def build():
    import os as _os
    STAGE = int(_os.environ.get("KSTAGE", "99"))
    nc = bass.Bass("TRN2", target_bir_lowering=False, debug=False, detect_race_conditions=False)
    p = Prog()

    x_in = nc.dram_tensor("x_tok", [TOK, D], F32, kind="ExternalInput").ap()
    wxp_in = nc.dram_tensor("wxp", [CH, D], F32, kind="ExternalInput").ap()
    alog_in = nc.dram_tensor("alog", [D, NST], F32, kind="ExternalInput").ap()
    dvec_in = nc.dram_tensor("dvec", [D, 1], F32, kind="ExternalInput").ap()
    w_ins = {
        "g": nc.dram_tensor("wg", [FF, D], F32, kind="ExternalInput").ap(),
        "u": nc.dram_tensor("wu", [FF, D], F32, kind="ExternalInput").ap(),
        "d": nc.dram_tensor("wd", [D, FF], F32, kind="ExternalInput").ap(),
    }
    idtb_in = nc.dram_tensor("idtb", [128, 128], BF16, kind="ExternalInput").ap()
    idtf_in = nc.dram_tensor("idtf", [128, 128], F32, kind="ExternalInput").ap()
    cm_in = nc.dram_tensor("cm", [128, NCORES], F32, kind="ExternalInput").ap()
    cm1_in = nc.dram_tensor("cm1", [128, NCORES], F32, kind="ExternalInput").ap()
    selm_in = nc.dram_tensor("selm", [32, 32 * 128], BF16, kind="ExternalInput").ap()
    out_hbm = nc.dram_tensor("out", [TOK, D], F32, kind="ExternalOutput").ap()
    KDBG = int(_os.environ.get("KDBG", "0"))
    if KDBG:
        dbg_big = nc.dram_tensor("dbg_big", [128, 8 * 1024], F32, kind="ExternalOutput").ap()
        dbg_small = nc.dram_tensor("dbg_small", [128, 160], F32, kind="ExternalOutput").ap()
        dbg_yat = nc.dram_tensor("dbg_yat", [TOK, D], F32, kind="ExternalOutput").ap()
        dbg_g16 = nc.dram_tensor("dbg_g16", [128, FTL * 512], BF16, kind="ExternalOutput").ap()
        dbg_slab = nc.dram_tensor("dbg_slab", [128, 8192], BF16, kind="ExternalOutput").ap()
        dbg_tu = nc.dram_tensor("dbg_tu", [512, 2048], BF16, kind="ExternalOutput").ap()

    wxpb = nc.dram_tensor("wxpb", [CH, D], BF16, kind="Internal").ap()
    tern = {
        "g": nc.dram_tensor("tg", [FF, D], BF16, kind="Internal").ap(),
        "u": nc.dram_tensor("tu", [FF, D], BF16, kind="Internal").ap(),
        "d": nc.dram_tensor("td", [D, FF], BF16, kind="Internal").ap(),
    }
    yat = nc.dram_tensor("yat", [TOK, D], F32, kind="Internal").ap()
    xqs_hbm = nc.dram_tensor("xqs", [128, KT * TOK], BF16, kind="Internal").ap()
    cc_in = nc.dram_tensor("cc_in", [128, 2 * KT * NST], F32, kind="Internal").ap()
    cc_out = nc.dram_tensor("cc_out", [NCORES, 128, 2 * KT * NST], F32, kind="Internal").ap()

    stack = ExitStack()
    import os as _os0
    _dbg = _os0.environ.get("KBUILD_DEBUG")
    def sb(name, shape, dt, st=None):
        h = (st or stack).enter_context(nc.sbuf_tensor(name, shape, dt))
        if _dbg:
            print("alloc", name, shape, "free:", nc.sbuf_bytes_remaining)
        return h
    psum = lambda name, shape, st=None: (st or stack).enter_context(nc.psum_tensor(name, shape, F32))

    idtb = sb("idtb_s", [128, 128], BF16)
    idtf = sb("idtf_s", [128, 128], F32)
    cmS = sb("cmS", [128, NCORES], F32)
    cm1S = sb("cm1S", [128, NCORES], F32)
    stats = sb("stats", [128, 80], F32)
    rowA = sb("rowA", [1, TOK], F32)
    wcolB = sb("wcolB", [128, 64], F32)
    thrS = sb("thrS", [128, 8], F32)
    swS = sb("swS", [1, 8], F32)
    srcol = sb("srcol", [128, TTL], F32)
    qscol = sb("qscol", [128, TTL], F32)
    escT = sb("escT", [128, TTL], F32)
    onesrow = sb("onesrow", [1, 128], F32)
    onesb = sb("onesb", [1, 128], BF16)
    onescol = sb("onescol", [128, 1], F32)
    onescb = sb("onescb", [128, 1], BF16)
    cst = sb("cst", [128, 8], F32)
    spcV = sb("spcV", [128, 40], F32)
    spcS = sb("spcS", [128, 40], F32)
    scf0 = sb("scf0", [128, 2048], F32)
    scf1 = sb("scf1", [128, 2048], F32)
    scb0 = sb("scb0", [128, 1024], BF16)
    scb1 = sb("scb1", [128, 1024], BF16)
    scb2 = sb("scb2", [128, 1024], BF16)
    scb3 = sb("scb3", [128, 1024], BF16)
    jnk = sb("jnk", [128, 1024], BF16)
    ibuf = sb("ibuf", [128, 512], I32)

    for s in SEMS:
        p.reg(s, stack.enter_context(nc.semaphore(s)))

    pT0 = stack.enter_context(nc.psum_tensor("pT0", [128, 128], BF16))
    pT1 = stack.enter_context(nc.psum_tensor("pT1", [128, 128], BF16))
    pB0 = psum("pB0", [128, 512])
    pB1 = psum("pB1", [128, 512])

    def vsp():
        p.emit("vector", lambda E: E.tensor_copy(spcV[:, :], spcV[:, :]))

    def ssp():
        p.emit("scalar", lambda E: E.copy(spcS[:, :], spcS[:, :]))

    def vspi(sem="VV"):
        return p.op("vector", lambda E: E.tensor_copy(spcV[:, :], spcV[:, :]), sem)

    def sspi(sem="SS"):
        return p.op("scalar", lambda E: E.copy(spcS[:, :], spcS[:, :]), sem)

    tr_state = {"evac": [0, 0]}

    def transpose_blk(src_ap, dst_ap, ident, ti, extra_guard=None):
        pt = (pT0, pT1)[ti % 2]
        p.wait_at("tensor", "VV", tr_state["evac"][ti % 2])
        tv = p.op("tensor", lambda E, o=pt, s=src_ap, idn=ident: E.transpose(o[:, :], s, idn[:, :]), "TN")
        p.wait_at("vector", "TN", tv)
        if extra_guard is not None:
            p.wait_at("vector", extra_guard[0], extra_guard[1])
        ev = p.op("vector", lambda E, o=dst_ap, i=pt: E.tensor_copy(o, i[:, :]), "VV")
        tr_state["evac"][ti % 2] = ev
        return ev

    bc_state = {"evac": [0, 0]}

    def bcast_half(lhs, row_ap_1xn, dst_ap_128xn, h, n=512):
        pb = (pB0, pB1)[h % 2]
        p.wait_at("tensor", "VV", p.val("VV"))
        tv = p.op("tensor", lambda E, pb=pb, r=row_ap_1xn, lhs=lhs, n=n: E.matmul(
            pb[:, 0:n], lhs[:, :], r, start=True, stop=True), "TN")
        p.wait_at("vector", "TN", tv)
        ev = p.op("vector", lambda E, o=dst_ap_128xn, pb=pb, n=n: E.tensor_copy(o, pb[:, 0:n]), "VV")
        bc_state["evac"][h % 2] = ev
        return ev

    def bcast_bc(i, hsl, dst_ap, h):
        pb = (pB0, pB1)[h % 2]
        p.wait_at("tensor", "VV", p.val("VV"))
        tv = p.op("tensor", lambda E, pb=pb, i=i, hsl=hsl: E.matmul(
            pb[:, 0:512], selm[:, i * 128:(i + 1) * 128], BCtb[:, hsl], start=True, stop=True), "TN")
        p.wait_at("vector", "TN", tv)
        ev = p.op("vector", lambda E, o=dst_ap, pb=pb: E.tensor_copy(o, pb[:, 0:512]), "VV")
        bc_state["evac"][h % 2] = ev
        return ev

    def row_to_col(row_ap, col_dst, nt):
        p.wait_at("tensor", "VV", p.val("VV"))
        for t in range(nt):
            pb = (pB0, pB1)[t % 2]
            p.wait_at("tensor", "VV", bc_state["evac"][t % 2])
            tv = p.op("tensor", lambda E, t=t, pb=pb, r=row_ap: E.matmul(
                pb[:, 0:1], r[0:1, t * 128:(t + 1) * 128], onesrow[0:1, 0:1], start=True, stop=True), "TN")
            p.wait_at("vector", "TN", tv)
            p.op("vector", lambda E, t=t, pb=pb, o=col_dst: E.tensor_copy(o[:, t:t + 1], pb[:, 0:1]))
            ev = vspi()
            bc_state["evac"][t % 2] = ev

    def col_to_row(col_ap, row_dst, nt):
        last = 0
        p.wait_at("tensor", "VV", p.val("VV"))
        for t in range(nt):
            pb = (pB0, pB1)[t % 2]
            p.wait_at("tensor", "VV", bc_state["evac"][t % 2])
            tv = p.op("tensor", lambda E, t=t, pb=pb, c=col_ap: E.matmul(
                pb[0:1, 0:128], c[:, t:t + 1], idtf[:, :], start=True, stop=True), "TN")
            p.wait_at("vector", "TN", tv)
            p.op("vector", lambda E, t=t, pb=pb, r=row_dst: E.tensor_copy(
                r[0:1, t * 128:(t + 1) * 128], pb[0:1, 0:128]))
            ev = vspi()
            bc_state["evac"][t % 2] = ev
            last = ev
        return last

    p.dma("sync", idtb[:, :], idtb_in)
    p.dma("sync", idtf[:, :], idtf_in)
    p.dma("sync", cmS[:, :], cm_in)
    p.dma("sync", cm1S[:, :], cm1_in)

    st1 = ExitStack()
    xnT = sb("xnT", [128, KT * TOK], BF16, st1)
    dt = sb("dt", [128, KT * TOK], BF16, st1)
    Acoef = sb("Acoef", [128, KT * NST], F32, st1)
    Dvec = sb("Dvec", [128, KT], F32, st1)
    HinA = sb("HinA", [128, KT * NST], F32, st1)
    HinB = sb("HinB", [128, KT * NST], F32, st1)
    ctots = sb("ctots", [128, KT], F32, st1)
    BCtb = sb("BCtb", [32, TOK], BF16, st1)
    selm = sb("selm_s", [32, 32 * 128], BF16, st1)
    st_ln = ExitStack()
    xn_tok = sb("xn_tok", [128, TTL * D], BF16, st_ln)
    xt0 = sb("xt0", [128, D], F32, st_ln)
    xt1 = sb("xt1", [128, D], F32, st_ln)

    for k in range(KT):
        p.dma("sync", Acoef[:, k * NST:(k + 1) * NST], alog_in[k * 128:(k + 1) * 128, :])
        p.dma("sync", Dvec[:, k:k + 1], dvec_in[k * 128:(k + 1) * 128, :])
    p.dma("sync", selm[:, :], selm_in)
    p.wait("vector", "LD")
    p.op("vector", lambda E: E.memset(onesrow[:, :], 1.0))
    p.op("vector", lambda E: E.memset(onesb[:, :], 1.0))
    p.op("vector", lambda E: E.memset(onescol[:, :], 1.0))
    p.op("vector", lambda E: E.memset(onescb[:, :], 1.0))
    p.op("vector", lambda E: E.memset(spcV[:, :], 0.0))
    p.op("scalar", lambda E: E.memzero(spcS[:, :]))
    p.op("vector", lambda E: E.memset(cst[:, 0:1], LN_EPS))
    p.op("vector", lambda E: E.memset(cst[:, 1:2], RMS_EPS))
    p.op("vector", lambda E: E.memset(cst[:, 2:3], 1.0))
    p.op("vector", lambda E: E.memset(cst[:, 3:4], -0.5))
    p.op("vector", lambda E: E.memset(cst[:, 4:5], -1.0))
    p.op("vector", lambda E: E.memset(cst[:, 5:6], 1.0 / D))
    p.op("vector", lambda E: E.memset(cst[:, 6:7], 1.0 / FF))
    p.op("vector", lambda E: E.memset(cst[:, 7:8], 0.5))
    p.op("vector", lambda E: E.memset(HinA[:, :], 0.0), "VV")
    p.wait("scalar", "LD")
    sv = p.op("scalar", lambda E: E.activation(scf0[:, 0:KT * NST], Acoef[:, :], AF.Exp, bias=0.0, scale=1.0), "SS")
    p.wait_at("vector", "SS", sv)
    p.op("vector", lambda E: E.tensor_scalar(Acoef[:, :], scf0[:, 0:KT * NST], -1.0, None, ALU.mult), "VV")

    xn_written = [0] * TTL
    for t in range(TTL):
        xt = (xt0, xt1)[t % 2]
        lxs = ("LX0", "LX1")[t % 2]
        if t >= 2:
            p.wait_at("sync", "VV", xn_written[t - 2])
        lv = p.dma("sync", xt[:, :], x_in[t * 128:(t + 1) * 128, :], lxs)
        p.wait_at("vector", lxs, lv)
        p.wait_at("scalar", lxs, lv)
        c = t * 8
        p.op("vector", lambda E, xt=xt, c=c: E.reduce_sum(stats[:, c:c + 1], xt[:, :], axis=AX), "VV")
        p.op("scalar", lambda E, xt=xt, c=c: E.activation(
            scf1[:, :], xt[:, :], AF.Square, bias=0.0, scale=1.0,
            accum_out=stats[:, c + 1:c + 2]))
        svv = sspi()
        p.wait_at("vector", "SS", svv)
        vsp()
        p.op("vector", lambda E, c=c: E.tensor_scalar(stats[:, c + 2:c + 3], stats[:, c:c + 1], 1.0 / D, None, ALU.mult))
        vsp()
        p.op("vector", lambda E, c=c: E.tensor_tensor(stats[:, c + 3:c + 4], stats[:, c + 2:c + 3], stats[:, c + 2:c + 3], ALU.mult))
        vsp()
        p.op("vector", lambda E, c=c: E.scalar_tensor_tensor(
            stats[:, c + 4:c + 5], stats[:, c + 1:c + 2], 1.0 / D, stats[:, c + 3:c + 4], ALU.mult, ALU.subtract))
        vv = vspi()
        p.wait_at("scalar", "VV", vv)
        p.op("scalar", lambda E, c=c: E.activation(stats[:, c + 5:c + 6], stats[:, c + 4:c + 5], AF.Ln, bias=cst[:, 0:1], scale=1.0))
        ssp()
        p.op("scalar", lambda E, c=c: E.activation(stats[:, c + 5:c + 6], stats[:, c + 5:c + 6], AF.Exp, bias=0.0, scale=cst[:, 3:4]))
        svv = sspi()
        p.wait_at("vector", "SS", svv)
        p.op("vector", lambda E, c=c: E.scalar_tensor_tensor(
            stats[:, c + 6:c + 7], stats[:, c + 2:c + 3], -1.0, stats[:, c + 5:c + 6], ALU.mult, ALU.mult))
        vsp()
        xn_written[t] = p.op("vector", lambda E, xt=xt, t=t, c=c: E.tensor_scalar(
            xn_tok[:, t * D:(t + 1) * D], xt[:, :],
            stats[:, c + 5:c + 6], stats[:, c + 6:c + 7], ALU.mult, ALU.add), "VV")

    p.wait("tensor", "VV")
    p.wait("tensor", "SS")
    for t in range(TTL):
        for k in range(KT):
            transpose_blk(xn_tok[:, t * D + k * 128: t * D + (k + 1) * 128],
                          xnT[:, k * TOK + t * 128: k * TOK + (t + 1) * 128], idtb, t * KT + k)

    p.wait("vector", "TN")
    p.wait("scalar", "TN")
    for t in range(TTL):
        c = t * 8
        p.op("scalar", lambda E, t=t, c=c: E.activation(
            scf1[:, :], xn_tok[:, t * D:(t + 1) * D], AF.Square, bias=0.0, scale=1.0,
            accum_out=stats[:, c:c + 1]))
        svv = sspi()
        vv = p.op("vector", lambda E, t=t, c=c: E.reduce_max(
            stats[:, c + 1:c + 2], xn_tok[:, t * D:(t + 1) * D], axis=AX, apply_absolute_value=True), "VV")
        p.wait_at("scalar", "SS", svv)
        p.wait_at("scalar", "VV", vv)
        ssp()
        p.op("scalar", lambda E, c=c: E.activation(stats[:, c + 2:c + 3], stats[:, c:c + 1], AF.Ln, bias=cst[:, 1:2], scale=cst[:, 5:6]))
        ssp()
        p.op("scalar", lambda E, c=c: E.activation(stats[:, c + 2:c + 3], stats[:, c + 2:c + 3], AF.Exp, bias=0.0, scale=cst[:, 3:4]))
        svv = sspi()
        p.wait_at("vector", "SS", svv)
        p.op("vector", lambda E, c=c: E.tensor_tensor(stats[:, c + 3:c + 4], stats[:, c + 1:c + 2], stats[:, c + 2:c + 3], ALU.mult))
        vsp()
        p.op("vector", lambda E, c=c: E.tensor_scalar(stats[:, c + 3:c + 4], stats[:, c + 3:c + 4], Q_EPS, None, ALU.max))
        vsp()
        p.op("vector", lambda E, c=c: E.reciprocal(stats[:, c + 4:c + 5], stats[:, c + 3:c + 4]))
        vsp()
        p.op("vector", lambda E, t=t, c=c: E.tensor_scalar(srcol[:, t:t + 1], stats[:, c + 4:c + 5], 127.0, None, ALU.mult))
        vsp()
        p.op("vector", lambda E, t=t, c=c: E.tensor_tensor(qscol[:, t:t + 1], stats[:, c + 2:c + 3], srcol[:, t:t + 1], ALU.mult))
        vsp()
        for hh in range(4):
            p.op("vector", lambda E, t=t, hh=hh: E.tensor_scalar(
                scf0[:, 0:512], xn_tok[:, t * D + hh * 512: t * D + (hh + 1) * 512],
                qscol[:, t:t + 1], None, ALU.mult))
            p.op("vector", lambda E: E.tensor_copy(ibuf[:, :], scf0[:, 0:512]))
            p.op("vector", lambda E, t=t, hh=hh: E.tensor_copy(
                xn_tok[:, t * D + hh * 512: t * D + (hh + 1) * 512], ibuf[:, :]), "VV")
    p.wait("tensor", "VV")
    xq_spill = [0, 0]
    for t in range(TTL):
        for k in range(KT):
            ti = t * KT + k
            sxs = ("SX0", "SX1")[ti % 2]
            stg = jnk[:, (ti % 2) * 128:(ti % 2) * 128 + 128]
            ev = transpose_blk(xn_tok[:, t * D + k * 128: t * D + (k + 1) * 128],
                               stg, idtb, ti, extra_guard=(sxs, xq_spill[ti % 2]))
            p.wait_at("gpsimd", "VV", ev)
            xq_spill[ti % 2] = p.dma("gpsimd", xqs_hbm[:, k * TOK + t * 128: k * TOK + (t + 1) * 128],
                                     stg, sxs)
    col_to_row(srcol[:, :], rowA[:, :], TTL)

    if STAGE <= 1:
        p.wait("gpsimd", "VV")
        p.wait("gpsimd", "SS")
        p.wait("gpsimd", "TN")
        return _assemble(nc, p, stack)
    p.wait("vector", "SX0")
    p.wait("vector", "SX1")
    cast_vv = [p.val("VV"), p.val("VV")]
    cast_st = 0
    for i in range(17):
        rows = 128 if i < 16 else 32
        lxs = ("LX0", "LX1")[i % 2]
        xt = (xt0, xt1)[i % 2]
        p.wait_at("sync", "VV", cast_vv[i % 2])
        lv = p.dma("sync", xt[0:rows, :], wxp_in[i * 128:i * 128 + rows, :], lxs)
        p.wait_at("vector", lxs, lv)
        p.wait_at("vector", "SP", cast_st)
        cast_vv[i % 2] = p.op("vector", lambda E, xt=xt, rows=rows: E.tensor_copy(xn_tok[0:rows, 0:2048], xt[0:rows, :]), "VV")
        p.wait_at("gpsimd", "VV", cast_vv[i % 2])
        cast_st = p.dma("gpsimd", wxpb[i * 128:i * 128 + rows, :], xn_tok[0:rows, 0:2048], "SP")

    p.wait("sync", "SP")
    p.wait("sync", "SX0")
    p.wait("sync", "SX1")
    st_ln.close()
    st_xp = ExitStack()
    wxpT = sb("wxpT", [128, KT * CH], BF16, st_xp)
    for k in range(KT):
        p.dmaT("sync", wxpT[:, k * CH:(k + 1) * CH], wxpb[0:CH, k * 128:(k + 1) * 128])
    p.barrier("tensor", "vector", "scalar")
    pX = [psum("pX%d" % i, [128, 512], st_xp) for i in range(4)]
    px_evac = [0, 0, 0, 0]
    for ci in range(17):
        chw = 128 if ci < 16 else 32
        for h in range(2):
            b = (2 * ci + h) % 4
            p.wait_at("tensor", "VV", px_evac[b])
            tv = 0
            for k in range(KT):
                tv = p.op("tensor", lambda E, ci=ci, h=h, k=k, b=b, chw=chw: E.matmul(
                    pX[b][0:chw, :], wxpT[:, k * CH + ci * 128: k * CH + ci * 128 + chw],
                    xnT[:, k * TOK + h * 512: k * TOK + (h + 1) * 512],
                    start=(k == 0), stop=(k == KT - 1)), "TN")
            if ci < 16:
                p.wait_at("scalar", "TN", tv)
                p.op("scalar", lambda E, b=b: E.activation(scf0[:, 0:512], pX[b][:, :], AF.Abs, bias=0.0, scale=1.0))
                p.op("scalar", lambda E: E.activation(scf0[:, 0:512], scf0[:, 0:512], AF.Exp, bias=0.0, scale=cst[:, 4:5]))
                p.op("scalar", lambda E: E.activation(scf0[:, 0:512], scf0[:, 0:512], AF.Ln, bias=cst[:, 2:3], scale=1.0))
                svv = p.op("scalar", lambda E, b=b: E.activation(scf1[:, 0:512], pX[b][:, :], AF.Relu, bias=0.0, scale=1.0), "SS")
                p.wait_at("vector", "SS", svv)
                px_evac[b] = p.op("vector", lambda E, ci=ci, h=h: E.tensor_tensor(
                    dt[:, ci * TOK + h * 512: ci * TOK + (h + 1) * 512],
                    scf0[:, 0:512], scf1[:, 0:512], ALU.add), "VV")
            else:
                p.wait_at("vector", "TN", tv)
                px_evac[b] = p.op("vector", lambda E, b=b, h=h: E.tensor_copy(
                    BCtb[:, h * 512:(h + 1) * 512], pX[b][0:32, :]), "VV")

    if STAGE <= 2:
        p.wait("gpsimd", "VV")
        p.wait("gpsimd", "SS")
        p.wait("gpsimd", "TN")
        return _assemble(nc, p, stack)
    p.barrier("sync", "gpsimd")
    st_xp.close()
    st_sc = ExitStack()
    u = sb("u", [128, KT * TOK], BF16, st_sc)
    st_ra = ExitStack()
    r_all = sb("r_all", [128, KT * TOK], BF16, st_ra)
    Bbc1 = sb("Bbc1", [128, TOK], BF16, st_ra)
    Svals = sb("Svals", [128, KT * NST], F32, st_ra)
    Pvals = sb("Pvals", [128, KT * NST], F32, st_ra)
    onesf = sb("onesf", [128, TOK], BF16, st_ra)
    p.wait("vector", "SS")
    p.op("vector", lambda E: E.memset(onesf[:, :], 1.0), "VV")
    pexp = 0
    for k in range(KT):
        p.op("vector", lambda E, k=k: E.tensor_tensor(
            u[:, k * TOK:(k + 1) * TOK], dt[:, k * TOK:(k + 1) * TOK], xnT[:, k * TOK:(k + 1) * TOK], ALU.mult))
        p.op("vector", lambda E, k=k: E.tensor_tensor_scan(
            scf0[:, 0:TOK], onesf[:, :], dt[:, k * TOK:(k + 1) * TOK], 0.0, ALU.mult, ALU.add))
        vsp()
        p.op("vector", lambda E, k=k: E.tensor_copy(ctots[:, k:k + 1], scf0[:, TOK - 1:TOK]))
        vsp()
        p.op("vector", lambda E, k=k: E.tensor_scalar(
            r_all[:, k * TOK:(k + 1) * TOK], scf0[:, 0:TOK], -1.0, ctots[:, k:k + 1], ALU.mult, ALU.add))
        p.wait_at("vector", "SS", pexp)
        p.op("vector", lambda E, k=k: E.tensor_scalar(
            scf1[:, 0:NST], Acoef[:, k * NST:(k + 1) * NST], ctots[:, k:k + 1], None, ALU.mult))
        vv = vspi()
        p.wait_at("scalar", "VV", vv)
        p.op("scalar", lambda E, k=k: E.activation(
            Pvals[:, k * NST:(k + 1) * NST], scf1[:, 0:NST], AF.Exp, bias=0.0, scale=1.0))
        pexp = sspi()

    if KDBG:
        p.wait("gpsimd", "SS")
        p.wait("gpsimd", "VV")
        p.dma("gpsimd", dbg_big[:, 4096:4352], Pvals[:, :], "ST")
        p.dma("gpsimd", dbg_big[:, 4352:4608], Acoef[:, :], "ST")
        p.dma("gpsimd", dbg_big[:, 4864:4880], ctots[:, :], "ST")
        p.wait("gpsimd", "ST")

    p.wait("vector", "SS")
    p.wait("tensor", "VV")
    stt_hist = []
    for n in range(NST):
        for h in range(2):
            bcast_bc(n, slice(h * 512, (h + 1) * 512), Bbc1[:, h * 512:(h + 1) * 512], h)
        for k in range(KT):
            it = n * KT + k
            bbuf = (scb0, scb2)[k % 2]
            Fbuf = (scb1, scb3)[k % 2]
            p.op("vector", lambda E, k=k, bbuf=bbuf: E.tensor_tensor(
                bbuf[:, 0:TOK], u[:, k * TOK:(k + 1) * TOK], Bbc1[:, :], ALU.mult))
            if it >= 2:
                p.wait_at("scalar", "VV", stt_hist[it - 2])
            svv = p.op("scalar", lambda E, k=k, n=n, Fbuf=Fbuf: E.activation(
                Fbuf[:, 0:TOK], r_all[:, k * TOK:(k + 1) * TOK], AF.Exp, bias=0.0,
                scale=Acoef[:, k * NST + n: k * NST + n + 1]), "SS")
            p.wait_at("vector", "SS", svv)
            stt_hist.append(p.op("vector", lambda E, k=k, n=n, Fbuf=Fbuf, bbuf=bbuf: E.scalar_tensor_tensor(
                jnk[:, 0:TOK], Fbuf[:, 0:TOK], 1.0, bbuf[:, 0:TOK], ALU.mult, ALU.mult,
                accum_out=Svals[:, k * NST + n: k * NST + n + 1]), "VV"))

    if STAGE <= 3:
        p.wait("gpsimd", "VV")
        p.wait("gpsimd", "SS")
        p.wait("gpsimd", "TN")
        return _assemble(nc, p, stack)
    p.wait("gpsimd", "VV")
    p.wait("gpsimd", "SS")
    p.dma("gpsimd", cc_in[:, 0:KT * NST], Svals[:, :], "SC")
    p.dma("gpsimd", cc_in[:, KT * NST:2 * KT * NST], Pvals[:, :], "SC")
    p.wait("gpsimd", "SC")
    ccv = p.op("gpsimd", lambda E: E.collective_compute(
        "AllGather", ALU.bypass, ins=[cc_in[:, :]], outs=[cc_out[:, :, :]],
        replica_groups=[list(range(NCORES))]), "CC")

    if STAGE <= 4:
        p.wait("gpsimd", "VV")
        p.wait("gpsimd", "SS")
        p.wait("gpsimd", "TN")
        return _assemble(nc, p, stack)
    p.barrier("scalar", "vector", "tensor")
    st_ra.close()
    st_tn = ExitStack()
    wt0 = sb("wt0", [128, 2048], F32, st_tn)
    wt1 = sb("wt1", [128, 2048], F32, st_tn)
    tscb2 = sb("tscb2", [128, 2048], BF16, st_tn)
    tscb3 = sb("tscb3", [128, 2048], BF16, st_tn)
    tscb4 = sb("tscb4", [128, 2048], BF16, st_tn)
    tscb5 = sb("tscb5", [128, 2048], BF16, st_tn)
    pW = psum("pW", [1, 8], st_tn)

    def tern_one(wi, w_in_ap, t_out_ap, layout):
        def tile_aps(i):
            if layout == "fd":
                r = i * 128
                return w_in_ap[r:r + 128, :], t_out_ap[r:r + 128, :]
            rr = (i // 4) * 128
            cc2 = (i % 4) * 2048
            return w_in_ap[rr:rr + 128, cc2:cc2 + 2048], t_out_ap[rr:rr + 128, cc2:cc2 + 2048]
        ntiles = 64
        p.wait("sync", "VV")
        p.wait("scalar", "SW0", "SW1")
        p.wait("scalar", "VV")
        abs_used = [0, 0]
        for i in range(ntiles):
            wt = (wt0, wt1)[i % 2]
            lws = ("LW0", "LW1")[i % 2]
            p.wait_at("sync", "SS", abs_used[i % 2])
            src, _ = tile_aps(i)
            v = p.dma("sync", wt[:, :], src, lws)
            p.wait_at("scalar", lws, v)
            p.op("scalar", lambda E, wt=wt, i=i: E.activation(
                (tscb5, tscb2)[i % 2][:, :], wt[:, :], AF.Abs, bias=0.0, scale=1.0,
                accum_out=wcolB[:, i:i + 1]))
            abs_used[i % 2] = sspi()
        svv = sspi()
        p.wait_at("vector", "SS", svv)
        p.op("vector", lambda E: E.reduce_sum(thrS[:, 7:8], wcolB[:, :], axis=AX))
        vv = vspi()
        p.wait_at("tensor", "VV", vv)
        tv = p.op("tensor", lambda E, wi=wi: E.matmul(
            pW[0:1, wi:wi + 1], onescol[:, 0:1], thrS[:, 7:8], start=True, stop=True), "TN")
        p.wait_at("vector", "TN", tv)
        p.op("vector", lambda E, wi=wi: E.tensor_scalar(
            swS[:, wi:wi + 1], pW[0:1, wi:wi + 1], 1.0 / (FF * D), Q_EPS, ALU.mult, ALU.max))
        vsp()
        p.op("vector", lambda E, wi=wi: E.tensor_scalar(
            swS[:, 4 + wi:5 + wi], swS[:, wi:wi + 1], 0.5, None, ALU.mult))
        vv = vspi()
        p.wait_at("tensor", "VV", max(vv, bc_state["evac"][0]))
        tv = p.op("tensor", lambda E, wi=wi: E.matmul(
            pB0[:, 0:1], onesrow[:, :], swS[0:1, 4 + wi:5 + wi], start=True, stop=True), "TN")
        p.wait_at("vector", "TN", tv)
        p.op("vector", lambda E, wi=wi: E.tensor_copy(thrS[:, 2 * wi:2 * wi + 1], pB0[:, 0:1]))
        vsp()
        p.op("vector", lambda E, wi=wi: E.tensor_scalar(
            thrS[:, 2 * wi + 1:2 * wi + 2], thrS[:, 2 * wi:2 * wi + 1], -1.0, None, ALU.mult))
        vv = vspi()
        bc_state["evac"][0] = vv
        p.wait("sync", "VV")
        tstore = [0, 0]
        tsub = [0, 0]
        for i in range(ntiles):
            wt = (wt0, wt1)[i % 2]
            g1 = (tscb5, tscb2)[i % 2]
            g2 = (tscb3, tscb4)[i % 2]
            lws = ("LW0", "LW1")[i % 2]
            sws = ("SW0", "SW1")[i % 2]
            p.wait_at("sync", "VV", tsub[i % 2])
            src, dst = tile_aps(i)
            v = p.dma("sync", wt[:, :], src, lws)
            p.wait_at("vector", lws, v)
            p.wait_at("vector", sws, tstore[i % 2])
            p.op("vector", lambda E, wt=wt, wi=wi, g1=g1: E.tensor_scalar(
                g1[:, :], wt[:, :], thrS[:, 2 * wi:2 * wi + 1], None, ALU.is_ge))
            p.op("vector", lambda E, wt=wt, wi=wi, g2=g2: E.tensor_scalar(
                g2[:, :], wt[:, :], thrS[:, 2 * wi + 1:2 * wi + 2], None, ALU.is_le))
            tsub[i % 2] = p.op("vector", lambda E, g1=g1, g2=g2: E.tensor_tensor(
                g1[:, :], g1[:, :], g2[:, :], ALU.subtract), "VV")
            p.wait_at("gpsimd", "VV", tsub[i % 2])
            tstore[i % 2] = p.dma("gpsimd", dst, g1[:, :], sws)

    p.wait("sync", "SC")
    tern_one(0, w_ins["g"], tern["g"], "fd")
    tern_one(1, w_ins["u"], tern["u"], "fd")
    tern_one(2, w_ins["d"], tern["d"], "df")
    p.wait("sync", "SW0", "SW1")
    p.wait("scalar", "SW0", "SW1")
    p.wait("vector", "SW0", "SW1")
    st_tn.close()

    if STAGE <= 5:
        p.wait("gpsimd", "VV")
        p.wait("gpsimd", "SS")
        p.wait("gpsimd", "TN")
        return _assemble(nc, p, stack)
    st_5c = ExitStack()
    st_g = ExitStack()
    gath = sb("gath", [128, NCORES * 2 * KT * NST], F32, st_g)
    p.wait_at("sync", "CC", ccv)
    for j in range(NCORES):
        p.dma("sync", gath[:, j * 512:(j + 1) * 512], cc_out[j, :, :])
    p.wait("vector", "LD")
    p.op("vector", lambda E: E.memset(HinA[:, :], 0.0))
    for j in range(NCORES - 1):
        Hsrc, Hdst = (HinA, HinB) if j % 2 == 0 else (HinB, HinA)
        base = j * 512
        p.op("vector", lambda E, base=base, j=j: E.tensor_scalar(
            scf0[:, 0:KT * NST], gath[:, base + KT * NST: base + 2 * KT * NST],
            cmS[:, j:j + 1], cm1S[:, j:j + 1], ALU.mult, ALU.add))
        p.op("vector", lambda E, base=base, j=j: E.tensor_scalar(
            scf1[:, 0:KT * NST], gath[:, base: base + KT * NST],
            cmS[:, j:j + 1], None, ALU.mult))
        p.op("vector", lambda E, Hsrc=Hsrc: E.tensor_tensor(
            scf0[:, 0:KT * NST], scf0[:, 0:KT * NST], Hsrc[:, :], ALU.mult))
        p.op("vector", lambda E, Hdst=Hdst: E.tensor_tensor(
            Hdst[:, :], scf0[:, 0:KT * NST], scf1[:, 0:KT * NST], ALU.add))
    Hin = HinB if (NCORES - 1) % 2 == 1 else HinA
    p.op("vector", lambda E: E.memset(scf0[:, 0:1], 0.0), "VV")
    if KDBG:
        p.wait("gpsimd", "VV")
        p.dma("gpsimd", dbg_big[:, 6144:6144 + 2048], gath[:, 0:2048], "ST")
        p.dma("gpsimd", dbg_big[:, 5120:5120 + 256], Hin[:, :], "ST")
        p.wait("gpsimd", "ST")
        p.wait("sync", "ST")
    p.wait("sync", "VV")
    st_g.close()
    Bbc = sb("Bbc", [128, NST * TOK], BF16, st_5c)
    Cbc = sb("Cbc", [128, NST * TOK], BF16, st_5c)

    p.wait("tensor", "VV")
    for n in range(NST):
        for (ri, dstt) in ((n, Bbc), (16 + n, Cbc)):
            for h in range(2):
                bcast_bc(ri, slice(h * 512, (h + 1) * 512),
                         dstt[:, n * TOK + h * 512: n * TOK + (h + 1) * 512], h)

    if STAGE <= 6:
        p.wait("gpsimd", "VV")
        p.wait("gpsimd", "SS")
        p.wait("gpsimd", "TN")
        return _assemble(nc, p, stack)
    pY = [psum("pY%d" % i, [128, 512], st_5c) for i in range(4)]
    py_evac = [0, 0]
    tts_done = [p.val("VV"), p.val("VV")]
    mm_done = [0, 0, 0, 0]
    spill_done = [0, 0]
    ytr_last = 0
    K5C = int(_os.environ.get("K5C_LIM", str(KT)))
    ytr_evac = [bc_state["evac"][0], bc_state["evac"][1]]
    SPILL = int(_os.environ.get("K5C_SPILL", "1"))
    for k in range(K5C):
        kb = k % 2
        p.wait_at("tensor", "VV", py_evac[kb])
        for n in range(NST):
            af = scf0[:, (n % 2) * TOK:((n % 2) + 1) * TOK]
            bbuf = (scb0, scb1)[n % 2]
            hb = (scb2, scb3)[n % 2]
            p.wait_at("scalar", "VV", tts_done[n % 2])
            svv = p.op("scalar", lambda E, k=k, n=n, af=af: E.activation(
                af, dt[:, k * TOK:(k + 1) * TOK], AF.Exp, bias=0.0,
                scale=Acoef[:, k * NST + n: k * NST + n + 1]), "SS")
            p.wait_at("vector", "TN", mm_done[2 * (n % 2)])
            p.wait_at("vector", "TN", mm_done[2 * (n % 2) + 1])
            p.op("vector", lambda E, k=k, n=n, bbuf=bbuf: E.tensor_tensor(
                bbuf[:, 0:TOK], u[:, k * TOK:(k + 1) * TOK], Bbc[:, n * TOK:(n + 1) * TOK], ALU.mult))
            p.wait_at("vector", "SS", svv)
            tts_done[n % 2] = p.op("vector", lambda E, k=k, n=n, af=af, bbuf=bbuf, hb=hb: E.tensor_tensor_scan(
                hb[:, 0:TOK], af, bbuf[:, 0:TOK],
                Hin[:, k * NST + n: k * NST + n + 1], ALU.mult, ALU.add), "VV")
            vv = p.op("vector", lambda E, k=k, n=n, hb=hb, bbuf=bbuf: E.tensor_tensor(
                bbuf[:, 0:TOK], hb[:, 0:TOK], Cbc[:, n * TOK:(n + 1) * TOK], ALU.mult), "VV")
            p.wait_at("tensor", "VV", vv)
            for h in range(2):
                mm_done[2 * (n % 2) + h] = p.op("tensor", lambda E, h=h, kb=kb, n=n, bbuf=bbuf: E.matmul(
                    pY[2 * kb + h][:, :], idtb[:, :], bbuf[:, h * 512:(h + 1) * 512],
                    start=(n == 0), stop=(n == NST - 1)), "TN")
        p.wait_at("vector", "TN", mm_done[2])
        p.wait_at("vector", "TN", mm_done[3])
        p.wait_at("vector", "TN", ytr_last)
        for h in range(2):
            p.op("vector", lambda E, k=k, h=h, kb=kb: E.scalar_tensor_tensor(
                scf1[:, h * 512:(h + 1) * 512], xnT[:, k * TOK + h * 512: k * TOK + (h + 1) * 512],
                Dvec[:, k:k + 1], pY[2 * kb + h][:, :], ALU.mult, ALU.add))
        yfold_vv = p.op("vector", lambda E: E.tensor_copy(ctots[:, 0:1], ctots[:, 0:1]), "VV")
        py_evac[kb] = yfold_vv
        if not SPILL:
            continue
        p.wait_at("tensor", "VV", yfold_vv)
        evs = []
        for t in range(TTL):
            pyb = (pB0, pB1)[t % 2]
            p.wait_at("tensor", "VV", ytr_evac[t % 2])
            tv = p.op("tensor", lambda E, t=t, pyb=pyb: E.transpose(
                pyb[:, 0:128], scf1[:, t * 128:(t + 1) * 128], idtf[:, :]), "TN")
            ytr_last = tv
            p.wait_at("vector", "TN", tv)
            sys_ = ("SY0", "SY1")[t % 2]
            p.wait_at("vector", sys_, spill_done[t % 2])
            ev = p.op("vector", lambda E, t=t, pyb=pyb: E.tensor_copy(
                scf1[:, TOK + t * 128: TOK + (t + 1) * 128], pyb[:, 0:128]), "VV")
            evs.append(ev)
            ytr_evac[t % 2] = ev
            bc_state["evac"][t % 2] = ev
            p.wait_at("gpsimd", "VV", ev)
            spill_done[t % 2] = p.dma("gpsimd", yat[t * 128:(t + 1) * 128, k * 128:(k + 1) * 128],
                                      scf1[:, TOK + t * 128: TOK + (t + 1) * 128], sys_)
        py_evac[kb] = evs[-1]

    if STAGE <= 7:
        p.wait("gpsimd", "VV")
        p.wait("gpsimd", "SS")
        p.wait("gpsimd", "TN")
        return _assemble(nc, p, stack)
    if KDBG:
        p.wait("gpsimd", "VV")
        p.wait("gpsimd", "SS")
        p.wait("gpsimd", "TN")
        p.op("vector", lambda E: E.tensor_copy(scf0[:, 0:TOK], dt[:, 0:TOK]))
        p.op("vector", lambda E: E.tensor_copy(scf0[:, TOK:2 * TOK], xnT[:, 0:TOK]), "VV")
        p.wait("gpsimd", "VV")
        p.dma("gpsimd", dbg_big[:, 0:1024], scf0[:, 0:TOK], "ST")
        p.dma("gpsimd", dbg_big[:, 1024:2048], scf0[:, TOK:2048], "ST")
        p.wait("vector", "ST")
        p.op("vector", lambda E: E.tensor_copy(scf0[:, 0:TOK], Bbc[:, 0:TOK]))
        p.op("vector", lambda E: E.tensor_copy(scf0[:, TOK:2 * TOK], Cbc[:, 0:TOK]), "VV")
        p.wait("gpsimd", "VV")
        p.dma("gpsimd", dbg_big[:, 2048:3072], scf0[:, 0:TOK], "ST")
        p.dma("gpsimd", dbg_big[:, 3072:4096], scf0[:, TOK:2048], "ST")
        p.dma("gpsimd", dbg_big[:, 4608:4864], Hin[:, :], "ST")
        p.wait("gpsimd", "ST")

    p.barrier("sync", "gpsimd", "vector", "scalar", "tensor")
    st_5c.close()
    st_sc.close()
    st1.close()
    st_ffn = ExitStack()
    NPH = 16
    THALF = TOK // 2
    NTT = THALF // 128
    g16 = sb("g16", [128, FTL * THALF], BF16, st_ffn)
    xqT = sb("xqT", [128, KT * THALF], BF16, st_ffn)
    slab = sb("slab", [128, KT * 512], BF16, st_ffn)
    slab2 = sb("slab2", [128, KT * 512], BF16, st_ffn)
    slab3 = sb("slab3", [128, KT * 512], BF16, st_ffn)
    ubuf = sb("ubuf", [128, THALF], BF16, st_ffn)
    mx0 = sb("mx0", [128, THALF], BF16, st_ffn)
    mx1 = sb("mx1", [128, THALF], BF16, st_ffn)
    sgcb = sb("sgcb", [128, THALF], F32, st_ffn)
    qscb = sb("qscb", [128, THALF], F32, st_ffn)
    tdT0 = sb("tdT0", [128, 512], BF16, st_ffn)
    tdT1 = sb("tdT1", [128, 512], BF16, st_ffn)
    xr0 = sb("xr0", [128, 512], F32, st_ffn)
    xr1 = sb("xr1", [128, 512], F32, st_ffn)
    yr0 = sb("yr0", [128, 512], F32, st_ffn)
    yr1 = sb("yr1", [128, 512], F32, st_ffn)
    ob0 = sb("ob0", [128, 512], F32, st_ffn)
    ob1 = sb("ob1", [128, 512], F32, st_ffn)
    pG = [psum("pG%d" % i, [128, 512], st_ffn) for i in range(4)]
    pD = pG
    pS = pB0

    p.op("vector", lambda E: E.reciprocal(rowA[0:1, :], rowA[0:1, :]))
    vsp()
    p.op("vector", lambda E: E.tensor_scalar(rowA[0:1, :], rowA[0:1, :], swS[0:1, 0:1], None, ALU.mult), "VV")
    p.op("vector", lambda E: E.tensor_copy(swS[:, 6:7], swS[:, 2:3]))
    vv = vspi()
    p.wait_at("tensor", "VV", p.val("VV"))
    tv = p.op("tensor", lambda E: E.matmul(pB0[:, 0:1], onesrow[:, :], swS[0:1, 6:7], start=True, stop=True), "TN")
    p.wait_at("vector", "TN", tv)
    p.op("vector", lambda E: E.tensor_copy(thrS[:, 6:7], pB0[:, 0:1]))
    vv = vspi()
    bc_state["evac"][0] = vv

    out_store = [0, 0]
    rl_used = [0, 0]
    for th in range(2):
        tbase = th * THALF
        p.wait("sync", "VV")
        p.wait("sync", "SS")
        p.wait("sync", "TN")
        for k in range(KT):
            p.dma("sync", xqT[:, k * THALF:(k + 1) * THALF],
                  xqs_hbm[:, k * TOK + tbase: k * TOK + tbase + THALF])
        bcast_half(onesrow, rowA[0:1, tbase:tbase + THALF], sgcb[:, :], 0, n=THALF)
        p.barrier("tensor", "vector", "scalar")

        def gu_pass(which):
            t_src = tern[which]
            pg_evac = [p.val("VV")] * 4
            pg_evac_s = [p.val("SS")] * 4
            prev_cons = 0
            slabs = (slab, slab2, slab3)
            lss = ("LR0", "LR1", "LS")
            slab_used = [p.val("TN")] * 3
            lv_arr = [0, 0, 0]

            def issue_loads(ph):
                par = ph % 3
                p.wait_at("sync", "TN", slab_used[par])
                lv = 0
                for k in range(KT):
                    lv = p.dmaT("sync", slabs[par][:, k * 512:(k + 1) * 512],
                                t_src[ph * 512:(ph + 1) * 512, k * 128:(k + 1) * 128], lss[par])
                lv_arr[par] = lv

            issue_loads(0)
            issue_loads(1)
            for ph in range(NPH):
                par = ph % 3
                p.wait_at("tensor", lss[par], lv_arr[par])
                if ph + 2 < NPH:
                    issue_loads(ph + 2)
                for fl in range(4):
                    fg = ph * 4 + fl
                    b = fl % 4
                    p.wait_at("tensor", "VV", pg_evac[b])
                    p.wait_at("tensor", "SS", pg_evac_s[b])
                    tv = 0
                    for k in range(KT):
                        tv = p.op("tensor", lambda E, k=k, fl=fl, b=b, par=par: E.matmul(
                            pG[b][:, :], slabs[par][:, k * 512 + fl * 128: k * 512 + (fl + 1) * 128],
                            xqT[:, k * THALF:(k + 1) * THALF],
                            start=(k == 0), stop=(k == KT - 1)), "TN")
                    if which == "g":
                        p.wait_at("vector", "TN", tv)
                        p.wait_at("vector", "SS", prev_cons)
                        vv = p.op("vector", lambda E, b=b: E.tensor_tensor(
                            scf0[:, 0:THALF], pG[b][:, :], sgcb[:, :], ALU.mult), "VV")
                        pg_evac[b] = vv
                        p.wait_at("scalar", "VV", vv)
                        sv2 = p.op("scalar", lambda E, fg=fg: E.activation(
                            g16[:, fg * THALF:(fg + 1) * THALF],
                            scf0[:, 0:THALF], AF.Silu, bias=0.0, scale=1.0), "SS")
                        pg_evac_s[b] = sv2
                        prev_cons = sv2
                    else:
                        p.wait_at("scalar", "TN", tv)
                        p.wait_at("scalar", "VV", prev_cons)
                        sv2 = p.op("scalar", lambda E, b=b: E.activation(
                            ubuf[:, :], pG[b][:, :], AF.Copy, bias=0.0, scale=1.0), "SS")
                        pg_evac_s[b] = sv2
                        p.wait_at("vector", "SS", sv2)
                        vv = p.op("vector", lambda E, fg=fg: E.tensor_tensor(
                            g16[:, fg * THALF:(fg + 1) * THALF],
                            g16[:, fg * THALF:(fg + 1) * THALF],
                            ubuf[:, :], ALU.mult), "VV")
                        pg_evac[b] = vv
                        prev_cons = vv
                slab_used[par] = p.val("TN")

        gu_pass("g")
        if KDBG and th == 1:
            p.wait("gpsimd", "SS")
            p.wait("gpsimd", "VV")
            p.dma("gpsimd", dbg_big[:, 1024:2048], g16[:, 62 * THALF:64 * THALF], "ST")
            p.wait("gpsimd", "ST")
            p.wait("sync", "ST")
            p.wait("scalar", "ST")
            p.wait("vector", "ST")
            p.wait("tensor", "ST")
        gu_pass("u")
        if KDBG and th == 1:
            p.wait("scalar", "TN")
            p.wait("scalar", "VV")
            p.op("scalar", lambda E: E.activation(mx1[:, :], pG[3][:, :], AF.Copy, bias=0.0, scale=1.0), "SS")
            p.op("scalar", lambda E: E.activation(mx0[:, :], pG[2][:, :], AF.Copy, bias=0.0, scale=1.0), "SS")
            p.wait("gpsimd", "SS")
            p.wait("gpsimd", "VV")
            p.dma("gpsimd", dbg_big[:, 2048:3072], g16[:, 62 * THALF:64 * THALF], "ST")
            p.dma("gpsimd", dbg_big[:, 3072:3584], ubuf[:, :], "ST")
            p.dma("gpsimd", dbg_big[:, 3584:4096], mx1[:, :], "ST")
            p.dma("gpsimd", dbg_big[:, 4096:4608], mx0[:, :], "ST")
            p.dma("gpsimd", dbg_slab[:, :], slab[:, :], "ST")
            p.dma("gpsimd", dbg_tu[:, :], tern["u"][7680:8192, :], "ST")
            p.wait("gpsimd", "ST")
            p.wait("sync", "ST")
            p.wait("scalar", "ST")
            p.wait("vector", "ST")
            p.wait("tensor", "ST")

        if KDBG and th == 1:
            p.wait("gpsimd", "VV")
            p.wait("gpsimd", "SS")
            p.dma("gpsimd", dbg_big[:, 0:1024], g16[:, 62 * THALF:64 * THALF], "ST")
            p.wait("gpsimd", "ST")
        p.barrier("tensor", "vector", "scalar")
        p.op("vector", lambda E: E.memset(mx0[:, :], 0.0), "VV")
        sq_mm = [0, 0]
        tree_vv = [0, 0]
        for f in range(FTL):
            sq = (scb0, scb1)[f % 2]
            p.wait_at("scalar", "TN", sq_mm[f % 2])
            p.wait_at("scalar", "VV", tree_vv[f % 2])
            svv = p.op("scalar", lambda E, f=f, sq=sq: E.activation(
                sq[:, 0:THALF], g16[:, f * THALF:(f + 1) * THALF], AF.Square, bias=0.0, scale=1.0), "SS")
            p.wait_at("tensor", "SS", svv)
            sq_mm[f % 2] = p.op("tensor", lambda E, f=f, sq=sq: E.matmul(
                pS[0:1, 0:THALF], onescb[:, 0:1], sq[:, 0:THALF],
                start=(f == 0), stop=(f == FTL - 1)), "TN")
            src_m = (mx0, mx1)[f % 2]
            dst_m = (mx1, mx0)[f % 2]
            p.wait_at("vector", "SS", svv)
            tree_vv[f % 2] = p.op("vector", lambda E, f=f, src_m=src_m, dst_m=dst_m, sq=sq: E.tensor_tensor(
                dst_m[:, :], src_m[:, :], sq[:, 0:THALF], ALU.max), "VV")
        mxf = (mx1, mx0)[(FTL - 1) % 2]
        p.wait("tensor", "VV")
        for t in range(NTT):
            pt = (pT0, pT1)[t % 2]
            p.wait_at("tensor", "VV", tr_state["evac"][t % 2])
            tv = p.op("tensor", lambda E, t=t, pt=pt, mxf=mxf: E.transpose(
                pt[:, :], mxf[:, t * 128:(t + 1) * 128], idtb[:, :]), "TN")
            p.wait_at("vector", "TN", tv)
            p.op("vector", lambda E, pt=pt: E.tensor_copy(scb0[:, 0:128], pt[:, :]))
            vsp()
            p.op("vector", lambda E, t=t, th=th: E.reduce_max(
                stats[:, 64 + th * NTT + t:65 + th * NTT + t], scb0[:, 0:128], axis=AX))
            ev = vspi()
            tr_state["evac"][t % 2] = ev
        p.wait("vector", "TN")
        p.op("vector", lambda E: E.tensor_copy(rowA[0:1, 0:THALF], pS[0:1, 0:THALF]), "VV")
        row_to_col(rowA[0:1, 0:THALF], stats[:, 72:72 + NTT], NTT)
        p.wait("scalar", "VV")
        p.op("scalar", lambda E: E.activation(scf0[:, 0:NTT], stats[:, 72:72 + NTT], AF.Ln, bias=cst[:, 1:2], scale=cst[:, 6:7]))
        ssp()
        p.op("scalar", lambda E: E.activation(scf0[:, 0:NTT], scf0[:, 0:NTT], AF.Exp, bias=0.0, scale=cst[:, 3:4]))
        ssp()
        p.op("scalar", lambda E, th=th: E.activation(scf0[:, NTT:2 * NTT], stats[:, 64 + th * NTT:64 + (th + 1) * NTT], AF.Ln, bias=0.0, scale=1.0))
        ssp()
        p.op("scalar", lambda E: E.activation(scf0[:, NTT:2 * NTT], scf0[:, NTT:2 * NTT], AF.Exp, bias=0.0, scale=cst[:, 7:8]))
        svv = sspi()
        p.wait_at("vector", "SS", svv)
        p.op("vector", lambda E: E.tensor_tensor(scf1[:, 0:NTT], scf0[:, NTT:2 * NTT], scf0[:, 0:NTT], ALU.mult))
        vsp()
        p.op("vector", lambda E: E.tensor_scalar(scf1[:, 0:NTT], scf1[:, 0:NTT], Q_EPS, None, ALU.max))
        vsp()
        p.op("vector", lambda E: E.reciprocal(scf1[:, NTT:2 * NTT], scf1[:, 0:NTT]))
        vsp()
        p.op("vector", lambda E: E.tensor_scalar(scf1[:, 0:NTT], scf1[:, NTT:2 * NTT], 127.0, None, ALU.mult))
        vsp()
        p.op("vector", lambda E: E.tensor_tensor(qscol[:, 0:NTT], scf0[:, 0:NTT], scf1[:, 0:NTT], ALU.mult))
        vsp()
        p.op("vector", lambda E: E.reciprocal(scf1[:, NTT:2 * NTT], scf1[:, 0:NTT]))
        vsp()
        p.op("vector", lambda E, th=th: E.tensor_scalar(
            escT[:, th * NTT:(th + 1) * NTT], scf1[:, NTT:2 * NTT], thrS[:, 6:7], None, ALU.mult))
        vv = vspi()
        vsp()
        cr = col_to_row(qscol[:, 0:NTT], rowA[0:1, 0:THALF], NTT)
        bcast_half(onesrow, rowA[0:1, 0:THALF], qscb[:, :], 0, n=THALF)

        p.wait("vector", "VV")
        for f in range(FTL):
            for hh in range(2):
                sl = slice(f * THALF + hh * 256, f * THALF + (hh + 1) * 256)
                p.op("vector", lambda E, sl=sl, hh=hh: E.tensor_tensor(
                    scf0[:, 0:256], g16[:, sl], qscb[:, hh * 256:(hh + 1) * 256], ALU.mult))
                p.op("vector", lambda E: E.tensor_copy(ibuf[:, 0:256], scf0[:, 0:256]))
                p.op("vector", lambda E, sl=sl: E.tensor_copy(g16[:, sl], ibuf[:, 0:256]), "VV")

        p.barrier("tensor", "vector", "scalar", "sync")
        dn_evac = 0
        for ds in range(4):
            slab_guard = [0, 0]
            p.wait_at("tensor", "VV", dn_evac)
            for f in range(FTL):
                tdT = (tdT0, tdT1)[f % 2]
                lts = ("LT0", "LT1")[f % 2]
                p.wait_at("sync", "TN", slab_guard[f % 2])
                lv = p.dmaT("sync", tdT[:, :], tern["d"][ds * 512:(ds + 1) * 512, f * 128:(f + 1) * 128], lts)
                p.wait_at("tensor", lts, lv)
                for t in range(NTT):
                    slab_guard[f % 2] = p.op("tensor", lambda E, f=f, t=t, tdT=tdT: E.matmul(
                        pD[t][:, :], g16[:, f * THALF + t * 128: f * THALF + (t + 1) * 128], tdT[:, :],
                        start=(f == 0), stop=(f == FTL - 1)), "TN")
            p.wait("vector", "TN")
            for t in range(NTT):
                tg_ = th * NTT + t
                xr = (xr0, xr1)[t % 2]
                yr = (yr0, yr1)[t % 2]
                ob = (ob0, ob1)[t % 2]
                lrs = ("LR0", "LR1")[t % 2]
                sos = ("SO0", "SO1")[t % 2]
                p.wait_at("sync", "VV", rl_used[t % 2])
                lx = p.dma("sync", xr[:, :], x_in[tg_ * 128:(tg_ + 1) * 128, ds * 512:(ds + 1) * 512], lrs)
                ly = p.dma("sync", yr[:, :], yat[tg_ * 128:(tg_ + 1) * 128, ds * 512:(ds + 1) * 512], lrs)
                p.wait_at("vector", lrs, ly)
                p.wait_at("vector", sos, out_store[t % 2])
                p.op("vector", lambda E, t=t, tg_=tg_, xr=xr, ob=ob: E.scalar_tensor_tensor(
                    ob[:, :], pD[t][:, :], escT[:, tg_:tg_ + 1], xr[:, :], ALU.mult, ALU.add))
                rl_used[t % 2] = p.op("vector", lambda E, ob=ob, yr=yr: E.tensor_tensor(
                    ob[:, :], ob[:, :], yr[:, :], ALU.add), "VV")
                p.wait_at("gpsimd", "VV", rl_used[t % 2])
                out_store[t % 2] = p.dma("gpsimd", out_hbm[tg_ * 128:(tg_ + 1) * 128, ds * 512:(ds + 1) * 512],
                                         ob[:, :], sos)
            dn_evac = p.val("VV")
    if KDBG:
        p.wait("gpsimd", "VV")
        p.wait("gpsimd", "SS")
        p.wait("gpsimd", "TN")
        for dsts, srcs in ((dbg_small[:, 0:8], escT[:, :]), (dbg_small[:, 8:16], qscol[:, :]),
                           (dbg_small[:, 16:24], srcol[:, :]), (dbg_small[:, 24:32], thrS[:, :]),
                           (dbg_small[:, 32:112], stats[:, :]), (dbg_small[:, 112:120], cst[:, :]),
                           (dbg_small[:, 120:121], sgcb[:, 0:1]), (dbg_small[:, 121:122], qscb[:, 0:1]),
                           (dbg_small[0:1, 122:130], swS[0:1, :])):
            p.dma("gpsimd", dsts, srcs, "ST", allow_slow_non_contiguous=True)
        p.dma("gpsimd", dbg_yat[:, :], yat[:, :], "ST")
        p.dma("gpsimd", dbg_g16[:, :], g16[:, :], "ST")
        p.wait("gpsimd", "ST")
    p.wait("gpsimd", "SO0")
    p.wait("gpsimd", "SO1")
    p.wait("sync", "SO0")
    p.wait("sync", "SO1")

    return _assemble(nc, p, stack)


def _assemble(nc, p, stack):
    with nc.Block() as block:
        @block.sync
        def _(E):
            for f in p.q["sync"]:
                f(E)

        @block.vector
        def _(E):
            for f in p.q["vector"]:
                f(E)

        @block.scalar
        def _(E):
            for f in p.q["scalar"]:
                f(E)

        @block.tensor
        def _(E):
            for f in p.q["tensor"]:
                f(E)

        @block.gpsimd
        def _(E):
            for f in p.q["gpsimd"]:
                f(E)

    return nc


_NC_CACHE = [None]


def kernel(**inputs):
    x = np.ascontiguousarray(np.asarray(inputs["L0_attn_x"], dtype=np.float32).reshape(8192, 2048))
    alog = np.ascontiguousarray(np.asarray(inputs["L0_attn_A_log"], dtype=np.float32))
    dv = np.ascontiguousarray(np.asarray(inputs["L0_attn_D"], dtype=np.float32).reshape(D, 1))
    wxp = np.ascontiguousarray(np.asarray(inputs["L0_attn_W_xproj"], dtype=np.float32))
    wg = np.ascontiguousarray(np.asarray(inputs["L0_ffn_w_gate"], dtype=np.float32))
    wu = np.ascontiguousarray(np.asarray(inputs["L0_ffn_w_up"], dtype=np.float32))
    wd = np.ascontiguousarray(np.asarray(inputs["L0_ffn_w_down"], dtype=np.float32))
    idb = np.eye(128, dtype=ml_dtypes.bfloat16)
    idf = np.eye(128, dtype=np.float32)
    selm = np.zeros((32, 32 * 128), dtype=ml_dtypes.bfloat16)
    for i in range(32):
        selm[i, i * 128:(i + 1) * 128] = 1.0

    if _NC_CACHE[0] is None:
        _NC_CACHE[0] = build()
    nc = _NC_CACHE[0]

    in_maps = []
    for c in range(NCORES):
        cm = np.zeros((128, NCORES), dtype=np.float32)
        cm[:, :c] = 1.0
        in_maps.append({
            "x_tok": x[c * TOK:(c + 1) * TOK],
            "wxp": wxp, "alog": alog, "dvec": dv,
            "wg": wg, "wu": wu, "wd": wd,
            "idtb": idb, "idtf": idf, "selm": selm,
            "cm": cm, "cm1": np.ascontiguousarray(1.0 - cm),
        })
    res = run_bass_kernel_spmd(nc, in_maps, core_ids=list(range(NCORES)))
    out = np.concatenate([res.results[c]["out"] for c in range(NCORES)], axis=0)
    return out.reshape(1, 8192, 2048)


# version 07 (final)
# speedup vs baseline: 1.0665x; 1.0665x over previous
import numpy as np
import ml_dtypes
from contextlib import ExitStack
import concourse.bass as bass
import concourse.mybir as mybir
from concourse.bass_utils import run_bass_kernel_spmd

F32 = mybir.dt.float32
BF16 = mybir.dt.bfloat16
I32 = mybir.dt.int32
ALU = mybir.AluOpType
AF = mybir.ActivationFunctionType
AX = mybir.AxisListType.X

TOK = 1024
D = 2048
NST = 16
CH = 2080
FF = 8192
NCORES = 8
KT = D // 128
TTL = TOK // 128
FTL = FF // 128
LN_EPS = 1e-5
RMS_EPS = 1e-6
Q_EPS = 1e-5
SEMS = ("LD", "ST", "VV", "SS", "TN", "CC",
        "LX0", "LX1", "LW0", "LW1", "LS", "LT0", "LT1", "LR0", "LR1",
        "SC", "SP", "SX0", "SX1", "SY0", "SY1", "SW0", "SW1", "SO0", "SO1")


class Prog:
    ENGS = ("sync", "vector", "scalar", "tensor", "gpsimd")

    def __init__(self):
        self.q = {e: [] for e in self.ENGS}
        self.sems = {}

    def reg(self, name, handle):
        self.sems[name] = [handle, 0]

    def emit(self, eng, f):
        self.q[eng].append(f)

    def val(self, name):
        return self.sems[name][1]

    def dma(self, eng, out, in_, sem="LD", **kw):
        h = self.sems[sem]
        h[1] += 16
        self.emit(eng, lambda E, o=out, i=in_, s=h[0], kw=kw: E.dma_start(out=o, in_=i, **kw).then_inc(s, 16))
        return h[1]

    def dmaT(self, eng, out, in_, sem="LD"):
        h = self.sems[sem]
        h[1] += 16
        self.emit(eng, lambda E, o=out, i=in_, s=h[0]: E.dma_start_transpose(o, i).then_inc(s, 16))
        return h[1]

    def wait(self, eng, *names):
        pairs = tuple((self.sems[n][0], self.sems[n][1]) for n in names if self.sems[n][1] > 0)

        def f(E, ps=pairs):
            for hh, v in ps:
                E.wait_ge(hh, v)
        self.emit(eng, f)

    def wait_at(self, eng, sem, v):
        if v <= 0:
            return
        h = self.sems[sem][0]
        self.emit(eng, lambda E, hh=h, vv=v: E.wait_ge(hh, vv))

    def op(self, eng, fn, sem=None):
        if sem is None:
            self.emit(eng, fn)
            return None
        h = self.sems[sem]
        h[1] += 1
        self.emit(eng, lambda E, f=fn, s=h[0]: f(E).then_inc(s, 1))
        return h[1]

    def barrier(self, *engs):
        for e in engs:
            self.wait(e, *SEMS)


def build():
    import os as _os
    STAGE = int(_os.environ.get("KSTAGE", "99"))
    nc = bass.Bass("TRN2", target_bir_lowering=False, debug=False, detect_race_conditions=False)
    p = Prog()

    x_in = nc.dram_tensor("x_tok", [TOK, D], F32, kind="ExternalInput").ap()
    wxp_in = nc.dram_tensor("wxp", [CH, D], F32, kind="ExternalInput").ap()
    alog_in = nc.dram_tensor("alog", [D, NST], F32, kind="ExternalInput").ap()
    dvec_in = nc.dram_tensor("dvec", [D, 1], F32, kind="ExternalInput").ap()
    w_ins = {
        "g": nc.dram_tensor("wg", [FF, D], F32, kind="ExternalInput").ap(),
        "u": nc.dram_tensor("wu", [FF, D], F32, kind="ExternalInput").ap(),
        "d": nc.dram_tensor("wd", [D, FF], F32, kind="ExternalInput").ap(),
    }
    idtb_in = nc.dram_tensor("idtb", [128, 128], BF16, kind="ExternalInput").ap()
    idtf_in = nc.dram_tensor("idtf", [128, 128], F32, kind="ExternalInput").ap()
    cm_in = nc.dram_tensor("cm", [128, NCORES], F32, kind="ExternalInput").ap()
    cm1_in = nc.dram_tensor("cm1", [128, NCORES], F32, kind="ExternalInput").ap()
    selm_in = nc.dram_tensor("selm", [32, 32 * 128], BF16, kind="ExternalInput").ap()
    out_hbm = nc.dram_tensor("out", [TOK, D], F32, kind="ExternalOutput").ap()
    KDBG = int(_os.environ.get("KDBG", "0"))
    if KDBG:
        dbg_big = nc.dram_tensor("dbg_big", [128, 8 * 1024], F32, kind="ExternalOutput").ap()
        dbg_small = nc.dram_tensor("dbg_small", [128, 160], F32, kind="ExternalOutput").ap()
        dbg_yat = nc.dram_tensor("dbg_yat", [TOK, D], F32, kind="ExternalOutput").ap()
        dbg_g16 = nc.dram_tensor("dbg_g16", [128, FTL * 512], BF16, kind="ExternalOutput").ap()
        dbg_slab = nc.dram_tensor("dbg_slab", [128, 8192], BF16, kind="ExternalOutput").ap()
        dbg_tu = nc.dram_tensor("dbg_tu", [512, 2048], BF16, kind="ExternalOutput").ap()

    wxpb = nc.dram_tensor("wxpb", [CH, D], BF16, kind="Internal").ap()
    tern = {
        "g": nc.dram_tensor("tg", [FF, D], BF16, kind="Internal").ap(),
        "u": nc.dram_tensor("tu", [FF, D], BF16, kind="Internal").ap(),
        "d": nc.dram_tensor("td", [D, FF], BF16, kind="Internal").ap(),
    }
    yat = nc.dram_tensor("yat", [TOK, D], F32, kind="Internal").ap()
    xqs_hbm = nc.dram_tensor("xqs", [128, KT * TOK], BF16, kind="Internal").ap()
    cc_in = nc.dram_tensor("cc_in", [128, 2 * KT * NST], F32, kind="Internal").ap()
    cc_out = nc.dram_tensor("cc_out", [NCORES, 128, 2 * KT * NST], F32, kind="Internal").ap()

    stack = ExitStack()
    import os as _os0
    _dbg = _os0.environ.get("KBUILD_DEBUG")
    def sb(name, shape, dt, st=None):
        h = (st or stack).enter_context(nc.sbuf_tensor(name, shape, dt))
        if _dbg:
            print("alloc", name, shape, "free:", nc.sbuf_bytes_remaining)
        return h
    psum = lambda name, shape, st=None: (st or stack).enter_context(nc.psum_tensor(name, shape, F32))

    idtb = sb("idtb_s", [128, 128], BF16)
    idtf = sb("idtf_s", [128, 128], F32)
    cmS = sb("cmS", [128, NCORES], F32)
    cm1S = sb("cm1S", [128, NCORES], F32)
    stats = sb("stats", [128, 80], F32)
    rowA = sb("rowA", [1, TOK], F32)
    wcolB = sb("wcolB", [128, 64], F32)
    thrS = sb("thrS", [128, 8], F32)
    swS = sb("swS", [1, 8], F32)
    srcol = sb("srcol", [128, TTL], F32)
    qscol = sb("qscol", [128, TTL], F32)
    escT = sb("escT", [128, TTL], F32)
    onesrow = sb("onesrow", [1, 128], F32)
    onesb = sb("onesb", [1, 128], BF16)
    onescol = sb("onescol", [128, 1], F32)
    onescb = sb("onescb", [128, 1], BF16)
    cst = sb("cst", [128, 8], F32)
    spcV = sb("spcV", [128, 40], F32)
    spcS = sb("spcS", [128, 40], F32)
    scf0 = sb("scf0", [128, 2048], F32)
    scf1 = sb("scf1", [128, 2048], F32)
    scb0 = sb("scb0", [128, 1024], BF16)
    scb1 = sb("scb1", [128, 1024], BF16)
    scb2 = sb("scb2", [128, 1024], BF16)
    scb3 = sb("scb3", [128, 1024], BF16)
    jnk = sb("jnk", [128, 1024], BF16)
    ibuf = sb("ibuf", [128, 512], I32)

    for s in SEMS:
        p.reg(s, stack.enter_context(nc.semaphore(s)))

    pT0 = stack.enter_context(nc.psum_tensor("pT0", [128, 128], BF16))
    pT1 = stack.enter_context(nc.psum_tensor("pT1", [128, 128], BF16))
    pB0 = psum("pB0", [128, 512])
    pB1 = psum("pB1", [128, 512])

    def vsp():
        p.emit("vector", lambda E: E.tensor_copy(spcV[:, :], spcV[:, :]))

    def ssp():
        p.emit("scalar", lambda E: E.copy(spcS[:, :], spcS[:, :]))

    def vspi(sem="VV"):
        return p.op("vector", lambda E: E.tensor_copy(spcV[:, :], spcV[:, :]), sem)

    def sspi(sem="SS"):
        return p.op("scalar", lambda E: E.copy(spcS[:, :], spcS[:, :]), sem)

    tr_state = {"evac": [0, 0]}

    def transpose_blk(src_ap, dst_ap, ident, ti, extra_guard=None):
        pt = (pT0, pT1)[ti % 2]
        p.wait_at("tensor", "VV", tr_state["evac"][ti % 2])
        tv = p.op("tensor", lambda E, o=pt, s=src_ap, idn=ident: E.transpose(o[:, :], s, idn[:, :]), "TN")
        p.wait_at("vector", "TN", tv)
        if extra_guard is not None:
            p.wait_at("vector", extra_guard[0], extra_guard[1])
        ev = p.op("vector", lambda E, o=dst_ap, i=pt: E.tensor_copy(o, i[:, :]), "VV")
        tr_state["evac"][ti % 2] = ev
        return ev

    bc_state = {"evac": [0, 0]}

    def bcast_half(lhs, row_ap_1xn, dst_ap_128xn, h, n=512):
        pb = (pB0, pB1)[h % 2]
        p.wait_at("tensor", "VV", p.val("VV"))
        tv = p.op("tensor", lambda E, pb=pb, r=row_ap_1xn, lhs=lhs, n=n: E.matmul(
            pb[:, 0:n], lhs[:, :], r, start=True, stop=True), "TN")
        p.wait_at("vector", "TN", tv)
        ev = p.op("vector", lambda E, o=dst_ap_128xn, pb=pb, n=n: E.tensor_copy(o, pb[:, 0:n]), "VV")
        bc_state["evac"][h % 2] = ev
        return ev

    def bcast_bc(i, hsl, dst_ap, h):
        pb = (pB0, pB1)[h % 2]
        p.wait_at("tensor", "VV", p.val("VV"))
        tv = p.op("tensor", lambda E, pb=pb, i=i, hsl=hsl: E.matmul(
            pb[:, 0:512], selm[:, i * 128:(i + 1) * 128], BCtb[:, hsl], start=True, stop=True), "TN")
        p.wait_at("vector", "TN", tv)
        ev = p.op("vector", lambda E, o=dst_ap, pb=pb: E.tensor_copy(o, pb[:, 0:512]), "VV")
        bc_state["evac"][h % 2] = ev
        return ev

    def row_to_col(row_ap, col_dst, nt):
        p.wait_at("tensor", "VV", p.val("VV"))
        for t in range(nt):
            pb = (pB0, pB1)[t % 2]
            p.wait_at("tensor", "VV", bc_state["evac"][t % 2])
            tv = p.op("tensor", lambda E, t=t, pb=pb, r=row_ap: E.matmul(
                pb[:, 0:1], r[0:1, t * 128:(t + 1) * 128], onesrow[0:1, 0:1], start=True, stop=True), "TN")
            p.wait_at("vector", "TN", tv)
            p.op("vector", lambda E, t=t, pb=pb, o=col_dst: E.tensor_copy(o[:, t:t + 1], pb[:, 0:1]))
            ev = vspi()
            bc_state["evac"][t % 2] = ev

    def col_to_row(col_ap, row_dst, nt):
        last = 0
        p.wait_at("tensor", "VV", p.val("VV"))
        for t in range(nt):
            pb = (pB0, pB1)[t % 2]
            p.wait_at("tensor", "VV", bc_state["evac"][t % 2])
            tv = p.op("tensor", lambda E, t=t, pb=pb, c=col_ap: E.matmul(
                pb[0:1, 0:128], c[:, t:t + 1], idtf[:, :], start=True, stop=True), "TN")
            p.wait_at("vector", "TN", tv)
            p.op("vector", lambda E, t=t, pb=pb, r=row_dst: E.tensor_copy(
                r[0:1, t * 128:(t + 1) * 128], pb[0:1, 0:128]))
            ev = vspi()
            bc_state["evac"][t % 2] = ev
            last = ev
        return last

    p.dma("sync", idtb[:, :], idtb_in)
    p.dma("sync", idtf[:, :], idtf_in)
    p.dma("sync", cmS[:, :], cm_in)
    p.dma("sync", cm1S[:, :], cm1_in)

    st1 = ExitStack()
    xnT = sb("xnT", [128, KT * TOK], BF16, st1)
    dt = sb("dt", [128, KT * TOK], BF16, st1)
    Acoef = sb("Acoef", [128, KT * NST], F32, st1)
    Dvec = sb("Dvec", [128, KT], F32, st1)
    HinA = sb("HinA", [128, KT * NST], F32, st1)
    HinB = sb("HinB", [128, KT * NST], F32, st1)
    ctots = sb("ctots", [128, KT], F32, st1)
    BCtb = sb("BCtb", [32, TOK], BF16, st1)
    selm = sb("selm_s", [32, 32 * 128], BF16, st1)
    st_ln = ExitStack()
    xn_tok = sb("xn_tok", [128, TTL * D], BF16, st_ln)
    xt0 = sb("xt0", [128, D], F32, st_ln)
    xt1 = sb("xt1", [128, D], F32, st_ln)

    for k in range(KT):
        p.dma("sync", Acoef[:, k * NST:(k + 1) * NST], alog_in[k * 128:(k + 1) * 128, :])
        p.dma("sync", Dvec[:, k:k + 1], dvec_in[k * 128:(k + 1) * 128, :])
    p.dma("sync", selm[:, :], selm_in)
    p.wait("vector", "LD")
    p.op("vector", lambda E: E.memset(onesrow[:, :], 1.0))
    p.op("vector", lambda E: E.memset(onesb[:, :], 1.0))
    p.op("vector", lambda E: E.memset(onescol[:, :], 1.0))
    p.op("vector", lambda E: E.memset(onescb[:, :], 1.0))
    p.op("vector", lambda E: E.memset(spcV[:, :], 0.0))
    p.op("scalar", lambda E: E.memzero(spcS[:, :]))
    p.op("vector", lambda E: E.memset(cst[:, 0:1], LN_EPS))
    p.op("vector", lambda E: E.memset(cst[:, 1:2], RMS_EPS))
    p.op("vector", lambda E: E.memset(cst[:, 2:3], 1.0))
    p.op("vector", lambda E: E.memset(cst[:, 3:4], -0.5))
    p.op("vector", lambda E: E.memset(cst[:, 4:5], -1.0))
    p.op("vector", lambda E: E.memset(cst[:, 5:6], 1.0 / D))
    p.op("vector", lambda E: E.memset(cst[:, 6:7], 1.0 / FF))
    p.op("vector", lambda E: E.memset(cst[:, 7:8], 0.5))
    p.op("vector", lambda E: E.memset(HinA[:, :], 0.0), "VV")
    p.wait("scalar", "LD")
    sv = p.op("scalar", lambda E: E.activation(scf0[:, 0:KT * NST], Acoef[:, :], AF.Exp, bias=0.0, scale=1.0), "SS")
    p.wait_at("vector", "SS", sv)
    p.op("vector", lambda E: E.tensor_scalar(Acoef[:, :], scf0[:, 0:KT * NST], -1.0, None, ALU.mult), "VV")

    xn_written = [0] * TTL
    for t in range(TTL):
        xt = (xt0, xt1)[t % 2]
        lxs = ("LX0", "LX1")[t % 2]
        if t >= 2:
            p.wait_at("sync", "VV", xn_written[t - 2])
        lv = p.dma("sync", xt[:, :], x_in[t * 128:(t + 1) * 128, :], lxs)
        p.wait_at("vector", lxs, lv)
        p.wait_at("scalar", lxs, lv)
        c = t * 8
        p.op("vector", lambda E, xt=xt, c=c: E.reduce_sum(stats[:, c:c + 1], xt[:, :], axis=AX), "VV")
        p.op("scalar", lambda E, xt=xt, c=c: E.activation(
            scf1[:, :], xt[:, :], AF.Square, bias=0.0, scale=1.0,
            accum_out=stats[:, c + 1:c + 2]))
        svv = sspi()
        p.wait_at("vector", "SS", svv)
        vsp()
        p.op("vector", lambda E, c=c: E.tensor_scalar(stats[:, c + 2:c + 3], stats[:, c:c + 1], 1.0 / D, None, ALU.mult))
        vsp()
        p.op("vector", lambda E, c=c: E.tensor_tensor(stats[:, c + 3:c + 4], stats[:, c + 2:c + 3], stats[:, c + 2:c + 3], ALU.mult))
        vsp()
        p.op("vector", lambda E, c=c: E.scalar_tensor_tensor(
            stats[:, c + 4:c + 5], stats[:, c + 1:c + 2], 1.0 / D, stats[:, c + 3:c + 4], ALU.mult, ALU.subtract))
        vv = vspi()
        p.wait_at("scalar", "VV", vv)
        p.op("scalar", lambda E, c=c: E.activation(stats[:, c + 5:c + 6], stats[:, c + 4:c + 5], AF.Ln, bias=cst[:, 0:1], scale=1.0))
        ssp()
        p.op("scalar", lambda E, c=c: E.activation(stats[:, c + 5:c + 6], stats[:, c + 5:c + 6], AF.Exp, bias=0.0, scale=cst[:, 3:4]))
        svv = sspi()
        p.wait_at("vector", "SS", svv)
        p.op("vector", lambda E, c=c: E.scalar_tensor_tensor(
            stats[:, c + 6:c + 7], stats[:, c + 2:c + 3], -1.0, stats[:, c + 5:c + 6], ALU.mult, ALU.mult))
        vsp()
        xn_written[t] = p.op("vector", lambda E, xt=xt, t=t, c=c: E.tensor_scalar(
            xn_tok[:, t * D:(t + 1) * D], xt[:, :],
            stats[:, c + 5:c + 6], stats[:, c + 6:c + 7], ALU.mult, ALU.add), "VV")

    p.wait("tensor", "VV")
    p.wait("tensor", "SS")
    for t in range(TTL):
        for k in range(KT):
            transpose_blk(xn_tok[:, t * D + k * 128: t * D + (k + 1) * 128],
                          xnT[:, k * TOK + t * 128: k * TOK + (t + 1) * 128], idtb, t * KT + k)

    p.wait("vector", "TN")
    p.wait("scalar", "TN")
    for t in range(TTL):
        c = t * 8
        p.op("scalar", lambda E, t=t, c=c: E.activation(
            scf1[:, :], xn_tok[:, t * D:(t + 1) * D], AF.Square, bias=0.0, scale=1.0,
            accum_out=stats[:, c:c + 1]))
        svv = sspi()
        vv = p.op("vector", lambda E, t=t, c=c: E.reduce_max(
            stats[:, c + 1:c + 2], xn_tok[:, t * D:(t + 1) * D], axis=AX, apply_absolute_value=True), "VV")
        p.wait_at("scalar", "SS", svv)
        p.wait_at("scalar", "VV", vv)
        ssp()
        p.op("scalar", lambda E, c=c: E.activation(stats[:, c + 2:c + 3], stats[:, c:c + 1], AF.Ln, bias=cst[:, 1:2], scale=cst[:, 5:6]))
        ssp()
        p.op("scalar", lambda E, c=c: E.activation(stats[:, c + 2:c + 3], stats[:, c + 2:c + 3], AF.Exp, bias=0.0, scale=cst[:, 3:4]))
        svv = sspi()
        p.wait_at("vector", "SS", svv)
        p.op("vector", lambda E, c=c: E.tensor_tensor(stats[:, c + 3:c + 4], stats[:, c + 1:c + 2], stats[:, c + 2:c + 3], ALU.mult))
        vsp()
        p.op("vector", lambda E, c=c: E.tensor_scalar(stats[:, c + 3:c + 4], stats[:, c + 3:c + 4], Q_EPS, None, ALU.max))
        vsp()
        p.op("vector", lambda E, c=c: E.reciprocal(stats[:, c + 4:c + 5], stats[:, c + 3:c + 4]))
        vsp()
        p.op("vector", lambda E, t=t, c=c: E.tensor_scalar(srcol[:, t:t + 1], stats[:, c + 4:c + 5], 127.0, None, ALU.mult))
        vsp()
        p.op("vector", lambda E, t=t, c=c: E.tensor_tensor(qscol[:, t:t + 1], stats[:, c + 2:c + 3], srcol[:, t:t + 1], ALU.mult))
        vsp()
        for hh in range(4):
            p.op("vector", lambda E, t=t, hh=hh: E.tensor_scalar(
                scf0[:, 0:512], xn_tok[:, t * D + hh * 512: t * D + (hh + 1) * 512],
                qscol[:, t:t + 1], None, ALU.mult))
            p.op("vector", lambda E: E.tensor_copy(ibuf[:, :], scf0[:, 0:512]))
            p.op("vector", lambda E, t=t, hh=hh: E.tensor_copy(
                xn_tok[:, t * D + hh * 512: t * D + (hh + 1) * 512], ibuf[:, :]), "VV")
    p.wait("tensor", "VV")
    xq_spill = [0, 0]
    for t in range(TTL):
        for k in range(KT):
            ti = t * KT + k
            sxs = ("SX0", "SX1")[ti % 2]
            stg = jnk[:, (ti % 2) * 128:(ti % 2) * 128 + 128]
            ev = transpose_blk(xn_tok[:, t * D + k * 128: t * D + (k + 1) * 128],
                               stg, idtb, ti, extra_guard=(sxs, xq_spill[ti % 2]))
            p.wait_at("gpsimd", "VV", ev)
            xq_spill[ti % 2] = p.dma("gpsimd", xqs_hbm[:, k * TOK + t * 128: k * TOK + (t + 1) * 128],
                                     stg, sxs)
    col_to_row(srcol[:, :], rowA[:, :], TTL)

    if STAGE <= 1:
        p.wait("gpsimd", "VV")
        p.wait("gpsimd", "SS")
        p.wait("gpsimd", "TN")
        return _assemble(nc, p, stack)
    p.wait("vector", "SX0")
    p.wait("vector", "SX1")
    cast_vv = [p.val("VV"), p.val("VV")]
    cast_st = 0
    for i in range(17):
        rows = 128 if i < 16 else 32
        lxs = ("LX0", "LX1")[i % 2]
        xt = (xt0, xt1)[i % 2]
        p.wait_at("sync", "VV", cast_vv[i % 2])
        lv = p.dma("sync", xt[0:rows, :], wxp_in[i * 128:i * 128 + rows, :], lxs)
        p.wait_at("vector", lxs, lv)
        p.wait_at("vector", "SP", cast_st)
        cast_vv[i % 2] = p.op("vector", lambda E, xt=xt, rows=rows: E.tensor_copy(xn_tok[0:rows, 0:2048], xt[0:rows, :]), "VV")
        p.wait_at("gpsimd", "VV", cast_vv[i % 2])
        cast_st = p.dma("gpsimd", wxpb[i * 128:i * 128 + rows, :], xn_tok[0:rows, 0:2048], "SP")

    p.wait("sync", "SP")
    p.wait("sync", "SX0")
    p.wait("sync", "SX1")
    st_ln.close()
    st_xp = ExitStack()
    wxpT = sb("wxpT", [128, KT * CH], BF16, st_xp)
    for k in range(KT):
        p.dmaT("sync", wxpT[:, k * CH:(k + 1) * CH], wxpb[0:CH, k * 128:(k + 1) * 128])
    p.barrier("tensor", "vector", "scalar")
    pX = [psum("pX%d" % i, [128, 512], st_xp) for i in range(4)]
    px_evac = [0, 0, 0, 0]
    for ci in range(17):
        chw = 128 if ci < 16 else 32
        for h in range(2):
            b = (2 * ci + h) % 4
            p.wait_at("tensor", "VV", px_evac[b])
            tv = 0
            for k in range(KT):
                tv = p.op("tensor", lambda E, ci=ci, h=h, k=k, b=b, chw=chw: E.matmul(
                    pX[b][0:chw, :], wxpT[:, k * CH + ci * 128: k * CH + ci * 128 + chw],
                    xnT[:, k * TOK + h * 512: k * TOK + (h + 1) * 512],
                    start=(k == 0), stop=(k == KT - 1)), "TN")
            if ci < 16:
                p.wait_at("scalar", "TN", tv)
                p.op("scalar", lambda E, b=b: E.activation(scf0[:, 0:512], pX[b][:, :], AF.Abs, bias=0.0, scale=1.0))
                p.op("scalar", lambda E: E.activation(scf0[:, 0:512], scf0[:, 0:512], AF.Exp, bias=0.0, scale=cst[:, 4:5]))
                p.op("scalar", lambda E: E.activation(scf0[:, 0:512], scf0[:, 0:512], AF.Ln, bias=cst[:, 2:3], scale=1.0))
                svv = p.op("scalar", lambda E, b=b: E.activation(scf1[:, 0:512], pX[b][:, :], AF.Relu, bias=0.0, scale=1.0), "SS")
                p.wait_at("vector", "SS", svv)
                px_evac[b] = p.op("vector", lambda E, ci=ci, h=h: E.tensor_tensor(
                    dt[:, ci * TOK + h * 512: ci * TOK + (h + 1) * 512],
                    scf0[:, 0:512], scf1[:, 0:512], ALU.add), "VV")
            else:
                p.wait_at("vector", "TN", tv)
                px_evac[b] = p.op("vector", lambda E, b=b, h=h: E.tensor_copy(
                    BCtb[:, h * 512:(h + 1) * 512], pX[b][0:32, :]), "VV")

    if STAGE <= 2:
        p.wait("gpsimd", "VV")
        p.wait("gpsimd", "SS")
        p.wait("gpsimd", "TN")
        return _assemble(nc, p, stack)
    p.barrier("sync", "gpsimd")
    st_xp.close()
    st_sc = ExitStack()
    u = sb("u", [128, KT * TOK], BF16, st_sc)
    st_ra = ExitStack()
    r_all = sb("r_all", [128, KT * TOK], BF16, st_ra)
    Bbc1 = sb("Bbc1", [128, TOK], BF16, st_ra)
    Svals = sb("Svals", [128, KT * NST], F32, st_ra)
    Pvals = sb("Pvals", [128, KT * NST], F32, st_ra)
    onesf = sb("onesf", [128, TOK], BF16, st_ra)
    p.wait("vector", "SS")
    p.op("vector", lambda E: E.memset(onesf[:, :], 1.0), "VV")
    pexp = 0
    for k in range(KT):
        p.op("vector", lambda E, k=k: E.tensor_tensor(
            u[:, k * TOK:(k + 1) * TOK], dt[:, k * TOK:(k + 1) * TOK], xnT[:, k * TOK:(k + 1) * TOK], ALU.mult))
        p.op("vector", lambda E, k=k: E.tensor_tensor_scan(
            scf0[:, 0:TOK], onesf[:, :], dt[:, k * TOK:(k + 1) * TOK], 0.0, ALU.mult, ALU.add))
        vsp()
        p.op("vector", lambda E, k=k: E.tensor_copy(ctots[:, k:k + 1], scf0[:, TOK - 1:TOK]))
        vsp()
        p.op("vector", lambda E, k=k: E.tensor_scalar(
            r_all[:, k * TOK:(k + 1) * TOK], scf0[:, 0:TOK], -1.0, ctots[:, k:k + 1], ALU.mult, ALU.add))
        p.wait_at("vector", "SS", pexp)
        p.op("vector", lambda E, k=k: E.tensor_scalar(
            scf1[:, 0:NST], Acoef[:, k * NST:(k + 1) * NST], ctots[:, k:k + 1], None, ALU.mult))
        vv = vspi()
        p.wait_at("scalar", "VV", vv)
        p.op("scalar", lambda E, k=k: E.activation(
            Pvals[:, k * NST:(k + 1) * NST], scf1[:, 0:NST], AF.Exp, bias=0.0, scale=1.0))
        pexp = sspi()

    if KDBG:
        p.wait("gpsimd", "SS")
        p.wait("gpsimd", "VV")
        p.dma("gpsimd", dbg_big[:, 4096:4352], Pvals[:, :], "ST")
        p.dma("gpsimd", dbg_big[:, 4352:4608], Acoef[:, :], "ST")
        p.dma("gpsimd", dbg_big[:, 4864:4880], ctots[:, :], "ST")
        p.wait("gpsimd", "ST")

    p.wait("vector", "SS")
    p.wait("tensor", "VV")
    stt_hist = []
    for n in range(NST):
        for h in range(2):
            bcast_bc(n, slice(h * 512, (h + 1) * 512), Bbc1[:, h * 512:(h + 1) * 512], h)
        for k in range(KT):
            it = n * KT + k
            bbuf = (scb0, scb2)[k % 2]
            Fbuf = (scb1, scb3)[k % 2]
            p.op("vector", lambda E, k=k, bbuf=bbuf: E.tensor_tensor(
                bbuf[:, 0:TOK], u[:, k * TOK:(k + 1) * TOK], Bbc1[:, :], ALU.mult))
            if it >= 2:
                p.wait_at("scalar", "VV", stt_hist[it - 2])
            svv = p.op("scalar", lambda E, k=k, n=n, Fbuf=Fbuf: E.activation(
                Fbuf[:, 0:TOK], r_all[:, k * TOK:(k + 1) * TOK], AF.Exp, bias=0.0,
                scale=Acoef[:, k * NST + n: k * NST + n + 1]), "SS")
            p.wait_at("vector", "SS", svv)
            stt_hist.append(p.op("vector", lambda E, k=k, n=n, Fbuf=Fbuf, bbuf=bbuf: E.scalar_tensor_tensor(
                jnk[:, 0:TOK], Fbuf[:, 0:TOK], 1.0, bbuf[:, 0:TOK], ALU.mult, ALU.mult,
                accum_out=Svals[:, k * NST + n: k * NST + n + 1]), "VV"))

    if STAGE <= 3:
        p.wait("gpsimd", "VV")
        p.wait("gpsimd", "SS")
        p.wait("gpsimd", "TN")
        return _assemble(nc, p, stack)
    p.wait("gpsimd", "VV")
    p.wait("gpsimd", "SS")
    p.dma("gpsimd", cc_in[:, 0:KT * NST], Svals[:, :], "SC")
    p.dma("gpsimd", cc_in[:, KT * NST:2 * KT * NST], Pvals[:, :], "SC")
    p.wait("gpsimd", "SC")
    ccv = p.op("gpsimd", lambda E: E.collective_compute(
        "AllGather", ALU.bypass, ins=[cc_in[:, :]], outs=[cc_out[:, :, :]],
        replica_groups=[list(range(NCORES))]), "CC")

    if STAGE <= 4:
        p.wait("gpsimd", "VV")
        p.wait("gpsimd", "SS")
        p.wait("gpsimd", "TN")
        return _assemble(nc, p, stack)
    p.barrier("scalar", "vector", "tensor")
    st_ra.close()
    st_tn = ExitStack()
    wt0 = sb("wt0", [128, 2048], F32, st_tn)
    wt1 = sb("wt1", [128, 2048], F32, st_tn)
    tscb2 = sb("tscb2", [128, 2048], BF16, st_tn)
    tscb3 = sb("tscb3", [128, 2048], BF16, st_tn)
    tscb4 = sb("tscb4", [128, 2048], BF16, st_tn)
    tscb5 = sb("tscb5", [128, 2048], BF16, st_tn)
    pW = psum("pW", [1, 8], st_tn)

    def tern_one(wi, w_in_ap, t_out_ap, layout):
        def tile_aps(i):
            if layout == "fd":
                r = i * 128
                return w_in_ap[r:r + 128, :], t_out_ap[r:r + 128, :]
            rr = (i // 4) * 128
            cc2 = (i % 4) * 2048
            return w_in_ap[rr:rr + 128, cc2:cc2 + 2048], t_out_ap[rr:rr + 128, cc2:cc2 + 2048]
        ntiles = 64
        p.wait("sync", "VV")
        p.wait("scalar", "SW0", "SW1")
        p.wait("scalar", "VV")
        abs_used = [0, 0]
        for i in range(ntiles):
            wt = (wt0, wt1)[i % 2]
            lws = ("LW0", "LW1")[i % 2]
            p.wait_at("sync", "SS", abs_used[i % 2])
            src, _ = tile_aps(i)
            v = p.dma("sync", wt[:, :], src, lws)
            p.wait_at("scalar", lws, v)
            p.op("scalar", lambda E, wt=wt, i=i: E.activation(
                (tscb5, tscb2)[i % 2][:, :], wt[:, :], AF.Abs, bias=0.0, scale=1.0,
                accum_out=wcolB[:, i:i + 1]))
            abs_used[i % 2] = sspi()
        svv = sspi()
        p.wait_at("vector", "SS", svv)
        p.op("vector", lambda E: E.reduce_sum(thrS[:, 7:8], wcolB[:, :], axis=AX))
        vv = vspi()
        p.wait_at("tensor", "VV", vv)
        tv = p.op("tensor", lambda E, wi=wi: E.matmul(
            pW[0:1, wi:wi + 1], onescol[:, 0:1], thrS[:, 7:8], start=True, stop=True), "TN")
        p.wait_at("vector", "TN", tv)
        p.op("vector", lambda E, wi=wi: E.tensor_scalar(
            swS[:, wi:wi + 1], pW[0:1, wi:wi + 1], 1.0 / (FF * D), Q_EPS, ALU.mult, ALU.max))
        vsp()
        p.op("vector", lambda E, wi=wi: E.tensor_scalar(
            swS[:, 4 + wi:5 + wi], swS[:, wi:wi + 1], 0.5, None, ALU.mult))
        vv = vspi()
        p.wait_at("tensor", "VV", max(vv, bc_state["evac"][0]))
        tv = p.op("tensor", lambda E, wi=wi: E.matmul(
            pB0[:, 0:1], onesrow[:, :], swS[0:1, 4 + wi:5 + wi], start=True, stop=True), "TN")
        p.wait_at("vector", "TN", tv)
        p.op("vector", lambda E, wi=wi: E.tensor_copy(thrS[:, 2 * wi:2 * wi + 1], pB0[:, 0:1]))
        vsp()
        p.op("vector", lambda E, wi=wi: E.tensor_scalar(
            thrS[:, 2 * wi + 1:2 * wi + 2], thrS[:, 2 * wi:2 * wi + 1], -1.0, None, ALU.mult))
        vv = vspi()
        bc_state["evac"][0] = vv
        p.wait("sync", "VV")
        tstore = [0, 0]
        tsub = [0, 0]
        for i in range(ntiles):
            wt = (wt0, wt1)[i % 2]
            g1 = (tscb5, tscb2)[i % 2]
            g2 = (tscb3, tscb4)[i % 2]
            lws = ("LW0", "LW1")[i % 2]
            sws = ("SW0", "SW1")[i % 2]
            p.wait_at("sync", "VV", tsub[i % 2])
            src, dst = tile_aps(i)
            v = p.dma("sync", wt[:, :], src, lws)
            p.wait_at("vector", lws, v)
            p.wait_at("vector", sws, tstore[i % 2])
            p.op("vector", lambda E, wt=wt, wi=wi, g1=g1: E.tensor_scalar(
                g1[:, :], wt[:, :], thrS[:, 2 * wi:2 * wi + 1], None, ALU.is_ge))
            p.op("vector", lambda E, wt=wt, wi=wi, g2=g2: E.tensor_scalar(
                g2[:, :], wt[:, :], thrS[:, 2 * wi + 1:2 * wi + 2], None, ALU.is_le))
            tsub[i % 2] = p.op("vector", lambda E, g1=g1, g2=g2: E.tensor_tensor(
                g1[:, :], g1[:, :], g2[:, :], ALU.subtract), "VV")
            p.wait_at("gpsimd", "VV", tsub[i % 2])
            tstore[i % 2] = p.dma("gpsimd", dst, g1[:, :], sws)

    p.wait("sync", "SC")
    tern_one(0, w_ins["g"], tern["g"], "fd")
    tern_one(1, w_ins["u"], tern["u"], "fd")
    tern_one(2, w_ins["d"], tern["d"], "df")
    p.wait("sync", "SW0", "SW1")
    p.wait("scalar", "SW0", "SW1")
    p.wait("vector", "SW0", "SW1")
    st_tn.close()

    if STAGE <= 5:
        p.wait("gpsimd", "VV")
        p.wait("gpsimd", "SS")
        p.wait("gpsimd", "TN")
        return _assemble(nc, p, stack)
    st_5c = ExitStack()
    st_g = ExitStack()
    gath = sb("gath", [128, NCORES * 2 * KT * NST], F32, st_g)
    p.wait_at("sync", "CC", ccv)
    for j in range(NCORES):
        p.dma("sync", gath[:, j * 512:(j + 1) * 512], cc_out[j, :, :])
    p.wait("vector", "LD")
    p.op("vector", lambda E: E.memset(HinA[:, :], 0.0))
    for j in range(NCORES - 1):
        Hsrc, Hdst = (HinA, HinB) if j % 2 == 0 else (HinB, HinA)
        base = j * 512
        p.op("vector", lambda E, base=base, j=j: E.tensor_scalar(
            scf0[:, 0:KT * NST], gath[:, base + KT * NST: base + 2 * KT * NST],
            cmS[:, j:j + 1], cm1S[:, j:j + 1], ALU.mult, ALU.add))
        p.op("vector", lambda E, base=base, j=j: E.tensor_scalar(
            scf1[:, 0:KT * NST], gath[:, base: base + KT * NST],
            cmS[:, j:j + 1], None, ALU.mult))
        p.op("vector", lambda E, Hsrc=Hsrc: E.tensor_tensor(
            scf0[:, 0:KT * NST], scf0[:, 0:KT * NST], Hsrc[:, :], ALU.mult))
        p.op("vector", lambda E, Hdst=Hdst: E.tensor_tensor(
            Hdst[:, :], scf0[:, 0:KT * NST], scf1[:, 0:KT * NST], ALU.add))
    Hin = HinB if (NCORES - 1) % 2 == 1 else HinA
    p.op("vector", lambda E: E.memset(scf0[:, 0:1], 0.0), "VV")
    if KDBG:
        p.wait("gpsimd", "VV")
        p.dma("gpsimd", dbg_big[:, 6144:6144 + 2048], gath[:, 0:2048], "ST")
        p.dma("gpsimd", dbg_big[:, 5120:5120 + 256], Hin[:, :], "ST")
        p.wait("gpsimd", "ST")
        p.wait("sync", "ST")
    p.wait("sync", "VV")
    st_g.close()
    Bbc = sb("Bbc", [128, NST * TOK], BF16, st_5c)
    Cbc = sb("Cbc", [128, NST * TOK], BF16, st_5c)

    p.wait("tensor", "VV")
    for n in range(NST):
        for (ri, dstt) in ((n, Bbc), (16 + n, Cbc)):
            for h in range(2):
                bcast_bc(ri, slice(h * 512, (h + 1) * 512),
                         dstt[:, n * TOK + h * 512: n * TOK + (h + 1) * 512], h)

    if STAGE <= 6:
        p.wait("gpsimd", "VV")
        p.wait("gpsimd", "SS")
        p.wait("gpsimd", "TN")
        return _assemble(nc, p, stack)
    pY = [psum("pY%d" % i, [128, 512], st_5c) for i in range(4)]
    py_evac = [0, 0]
    tts_done = [p.val("VV"), p.val("VV")]
    mm_done = [0, 0, 0, 0]
    spill_done = [0, 0]
    ytr_last = 0
    K5C = int(_os.environ.get("K5C_LIM", str(KT)))
    ytr_evac = [bc_state["evac"][0], bc_state["evac"][1]]
    SPILL = int(_os.environ.get("K5C_SPILL", "1"))
    for k in range(K5C):
        kb = k % 2
        p.wait_at("tensor", "VV", py_evac[kb])
        for n in range(NST):
            af = scf0[:, (n % 2) * TOK:((n % 2) + 1) * TOK]
            bbuf = (scb0, scb1)[n % 2]
            hb = (scb2, scb3)[n % 2]
            p.wait_at("scalar", "VV", tts_done[n % 2])
            svv = p.op("scalar", lambda E, k=k, n=n, af=af: E.activation(
                af, dt[:, k * TOK:(k + 1) * TOK], AF.Exp, bias=0.0,
                scale=Acoef[:, k * NST + n: k * NST + n + 1]), "SS")
            p.wait_at("vector", "TN", mm_done[2 * (n % 2)])
            p.wait_at("vector", "TN", mm_done[2 * (n % 2) + 1])
            p.op("vector", lambda E, k=k, n=n, bbuf=bbuf: E.tensor_tensor(
                bbuf[:, 0:TOK], u[:, k * TOK:(k + 1) * TOK], Bbc[:, n * TOK:(n + 1) * TOK], ALU.mult))
            p.wait_at("vector", "SS", svv)
            tts_done[n % 2] = p.op("vector", lambda E, k=k, n=n, af=af, bbuf=bbuf, hb=hb: E.tensor_tensor_scan(
                hb[:, 0:TOK], af, bbuf[:, 0:TOK],
                Hin[:, k * NST + n: k * NST + n + 1], ALU.mult, ALU.add), "VV")
            vv = p.op("vector", lambda E, k=k, n=n, hb=hb, bbuf=bbuf: E.tensor_tensor(
                bbuf[:, 0:TOK], hb[:, 0:TOK], Cbc[:, n * TOK:(n + 1) * TOK], ALU.mult), "VV")
            p.wait_at("tensor", "VV", vv)
            for h in range(2):
                mm_done[2 * (n % 2) + h] = p.op("tensor", lambda E, h=h, kb=kb, n=n, bbuf=bbuf: E.matmul(
                    pY[2 * kb + h][:, :], idtb[:, :], bbuf[:, h * 512:(h + 1) * 512],
                    start=(n == 0), stop=(n == NST - 1)), "TN")
        p.wait_at("vector", "TN", mm_done[2])
        p.wait_at("vector", "TN", mm_done[3])
        p.wait_at("vector", "TN", ytr_last)
        for h in range(2):
            p.op("vector", lambda E, k=k, h=h, kb=kb: E.scalar_tensor_tensor(
                scf1[:, h * 512:(h + 1) * 512], xnT[:, k * TOK + h * 512: k * TOK + (h + 1) * 512],
                Dvec[:, k:k + 1], pY[2 * kb + h][:, :], ALU.mult, ALU.add))
        yfold_vv = p.op("vector", lambda E: E.tensor_copy(ctots[:, 0:1], ctots[:, 0:1]), "VV")
        py_evac[kb] = yfold_vv
        if not SPILL:
            continue
        p.wait_at("tensor", "VV", yfold_vv)
        evs = []
        for t in range(TTL):
            pyb = (pB0, pB1)[t % 2]
            p.wait_at("tensor", "VV", ytr_evac[t % 2])
            tv = p.op("tensor", lambda E, t=t, pyb=pyb: E.transpose(
                pyb[:, 0:128], scf1[:, t * 128:(t + 1) * 128], idtf[:, :]), "TN")
            ytr_last = tv
            p.wait_at("vector", "TN", tv)
            sys_ = ("SY0", "SY1")[t % 2]
            p.wait_at("vector", sys_, spill_done[t % 2])
            ev = p.op("vector", lambda E, t=t, pyb=pyb: E.tensor_copy(
                scf1[:, TOK + t * 128: TOK + (t + 1) * 128], pyb[:, 0:128]), "VV")
            evs.append(ev)
            ytr_evac[t % 2] = ev
            bc_state["evac"][t % 2] = ev
            p.wait_at("gpsimd", "VV", ev)
            spill_done[t % 2] = p.dma("gpsimd", yat[t * 128:(t + 1) * 128, k * 128:(k + 1) * 128],
                                      scf1[:, TOK + t * 128: TOK + (t + 1) * 128], sys_)
        py_evac[kb] = evs[-1]

    if STAGE <= 7:
        p.wait("gpsimd", "VV")
        p.wait("gpsimd", "SS")
        p.wait("gpsimd", "TN")
        return _assemble(nc, p, stack)
    if KDBG:
        p.wait("gpsimd", "VV")
        p.wait("gpsimd", "SS")
        p.wait("gpsimd", "TN")
        p.op("vector", lambda E: E.tensor_copy(scf0[:, 0:TOK], dt[:, 0:TOK]))
        p.op("vector", lambda E: E.tensor_copy(scf0[:, TOK:2 * TOK], xnT[:, 0:TOK]), "VV")
        p.wait("gpsimd", "VV")
        p.dma("gpsimd", dbg_big[:, 0:1024], scf0[:, 0:TOK], "ST")
        p.dma("gpsimd", dbg_big[:, 1024:2048], scf0[:, TOK:2048], "ST")
        p.wait("vector", "ST")
        p.op("vector", lambda E: E.tensor_copy(scf0[:, 0:TOK], Bbc[:, 0:TOK]))
        p.op("vector", lambda E: E.tensor_copy(scf0[:, TOK:2 * TOK], Cbc[:, 0:TOK]), "VV")
        p.wait("gpsimd", "VV")
        p.dma("gpsimd", dbg_big[:, 2048:3072], scf0[:, 0:TOK], "ST")
        p.dma("gpsimd", dbg_big[:, 3072:4096], scf0[:, TOK:2048], "ST")
        p.dma("gpsimd", dbg_big[:, 4608:4864], Hin[:, :], "ST")
        p.wait("gpsimd", "ST")

    p.barrier("sync", "gpsimd", "vector", "scalar", "tensor")
    st_5c.close()
    st_sc.close()
    st1.close()
    st_ffn = ExitStack()
    NPH = 16
    THALF = TOK // 2
    NTT = THALF // 128
    g16 = sb("g16", [128, FTL * THALF], BF16, st_ffn)
    xqT = sb("xqT", [128, KT * THALF], BF16, st_ffn)
    slab = sb("slab", [128, KT * 1024], BF16, st_ffn)
    slab2 = sb("slab2", [128, KT * 1024], BF16, st_ffn)
    ubuf = sb("ubuf", [128, THALF], BF16, st_ffn)
    mx0 = sb("mx0", [128, THALF], BF16, st_ffn)
    mx1 = sb("mx1", [128, THALF], BF16, st_ffn)
    sgcb = sb("sgcb", [128, THALF], F32, st_ffn)
    qscb = sb("qscb", [128, THALF], F32, st_ffn)
    tdT0 = sb("tdT0", [128, 512], BF16, st_ffn)
    tdT1 = sb("tdT1", [128, 512], BF16, st_ffn)
    xr0 = sb("xr0", [128, 512], F32, st_ffn)
    xr1 = sb("xr1", [128, 512], F32, st_ffn)
    yr0 = sb("yr0", [128, 512], F32, st_ffn)
    yr1 = sb("yr1", [128, 512], F32, st_ffn)
    ob0 = sb("ob0", [128, 512], F32, st_ffn)
    ob1 = sb("ob1", [128, 512], F32, st_ffn)
    pG = [psum("pG%d" % i, [128, 512], st_ffn) for i in range(4)]
    pD = pG
    pS = pB0

    p.op("vector", lambda E: E.reciprocal(rowA[0:1, :], rowA[0:1, :]))
    vsp()
    p.op("vector", lambda E: E.tensor_scalar(rowA[0:1, :], rowA[0:1, :], swS[0:1, 0:1], None, ALU.mult), "VV")
    p.op("vector", lambda E: E.tensor_copy(swS[:, 6:7], swS[:, 2:3]))
    vv = vspi()
    p.wait_at("tensor", "VV", p.val("VV"))
    tv = p.op("tensor", lambda E: E.matmul(pB0[:, 0:1], onesrow[:, :], swS[0:1, 6:7], start=True, stop=True), "TN")
    p.wait_at("vector", "TN", tv)
    p.op("vector", lambda E: E.tensor_copy(thrS[:, 6:7], pB0[:, 0:1]))
    vv = vspi()
    bc_state["evac"][0] = vv

    out_store = [0, 0]
    rl_used = [0, 0]
    for th in range(2):
        tbase = th * THALF
        p.wait("sync", "VV")
        p.wait("sync", "SS")
        p.wait("sync", "TN")
        for k in range(KT):
            p.dma("sync", xqT[:, k * THALF:(k + 1) * THALF],
                  xqs_hbm[:, k * TOK + tbase: k * TOK + tbase + THALF])
        bcast_half(onesrow, rowA[0:1, tbase:tbase + THALF], sgcb[:, :], 0, n=THALF)
        p.barrier("tensor", "vector", "scalar")

        def gu_pass(which):
            t_src = tern[which]
            pg_evac = [p.val("VV")] * 4
            pg_evac_s = [p.val("SS")] * 4
            prev_cons = 0
            slabs = (slab, slab2)
            lss = ("LR0", "LR1")
            slab_used = [p.val("TN"), p.val("TN")]
            lv_arr = [0, 0]

            def issue_loads(pp):
                par = pp % 2
                p.wait_at("sync", "TN", slab_used[par])
                lv = 0
                for k in range(KT):
                    lv = p.dmaT("sync", slabs[par][:, k * 1024:(k + 1) * 1024],
                                t_src[pp * 1024:(pp + 1) * 1024, k * 128:(k + 1) * 128], lss[par])
                lv_arr[par] = lv

            issue_loads(0)
            for pp in range(NPH // 2):
                par = pp % 2
                p.wait_at("tensor", lss[par], lv_arr[par])
                if pp + 1 < NPH // 2:
                    issue_loads(pp + 1)
                for fl in range(8):
                    fg = pp * 8 + fl
                    b = fl % 4
                    p.wait_at("tensor", "VV", pg_evac[b])
                    p.wait_at("tensor", "SS", pg_evac_s[b])
                    tv = 0
                    for k in range(KT):
                        tv = p.op("tensor", lambda E, k=k, fl=fl, b=b, par=par: E.matmul(
                            pG[b][:, :], slabs[par][:, k * 1024 + fl * 128: k * 1024 + (fl + 1) * 128],
                            xqT[:, k * THALF:(k + 1) * THALF],
                            start=(k == 0), stop=(k == KT - 1)), "TN")
                    if which == "g":
                        p.wait_at("vector", "TN", tv)
                        p.wait_at("vector", "SS", prev_cons)
                        vv = p.op("vector", lambda E, b=b: E.tensor_tensor(
                            scf0[:, 0:THALF], pG[b][:, :], sgcb[:, :], ALU.mult), "VV")
                        pg_evac[b] = vv
                        p.wait_at("scalar", "VV", vv)
                        sv2 = p.op("scalar", lambda E, fg=fg: E.activation(
                            g16[:, fg * THALF:(fg + 1) * THALF],
                            scf0[:, 0:THALF], AF.Silu, bias=0.0, scale=1.0), "SS")
                        pg_evac_s[b] = sv2
                        prev_cons = sv2
                    else:
                        p.wait_at("scalar", "TN", tv)
                        p.wait_at("scalar", "VV", prev_cons)
                        sv2 = p.op("scalar", lambda E, b=b: E.activation(
                            ubuf[:, :], pG[b][:, :], AF.Copy, bias=0.0, scale=1.0), "SS")
                        pg_evac_s[b] = sv2
                        p.wait_at("vector", "SS", sv2)
                        vv = p.op("vector", lambda E, fg=fg: E.tensor_tensor(
                            g16[:, fg * THALF:(fg + 1) * THALF],
                            g16[:, fg * THALF:(fg + 1) * THALF],
                            ubuf[:, :], ALU.mult), "VV")
                        pg_evac[b] = vv
                        prev_cons = vv
                slab_used[par] = p.val("TN")

        gu_pass("g")
        if KDBG and th == 1:
            p.wait("gpsimd", "SS")
            p.wait("gpsimd", "VV")
            p.dma("gpsimd", dbg_big[:, 1024:2048], g16[:, 62 * THALF:64 * THALF], "ST")
            p.wait("gpsimd", "ST")
            p.wait("sync", "ST")
            p.wait("scalar", "ST")
            p.wait("vector", "ST")
            p.wait("tensor", "ST")
        gu_pass("u")
        if KDBG and th == 1:
            p.wait("scalar", "TN")
            p.wait("scalar", "VV")
            p.op("scalar", lambda E: E.activation(mx1[:, :], pG[3][:, :], AF.Copy, bias=0.0, scale=1.0), "SS")
            p.op("scalar", lambda E: E.activation(mx0[:, :], pG[2][:, :], AF.Copy, bias=0.0, scale=1.0), "SS")
            p.wait("gpsimd", "SS")
            p.wait("gpsimd", "VV")
            p.dma("gpsimd", dbg_big[:, 2048:3072], g16[:, 62 * THALF:64 * THALF], "ST")
            p.dma("gpsimd", dbg_big[:, 3072:3584], ubuf[:, :], "ST")
            p.dma("gpsimd", dbg_big[:, 3584:4096], mx1[:, :], "ST")
            p.dma("gpsimd", dbg_big[:, 4096:4608], mx0[:, :], "ST")
            p.dma("gpsimd", dbg_slab[:, :], slab[:, :], "ST")
            p.dma("gpsimd", dbg_tu[:, :], tern["u"][7680:8192, :], "ST")
            p.wait("gpsimd", "ST")
            p.wait("sync", "ST")
            p.wait("scalar", "ST")
            p.wait("vector", "ST")
            p.wait("tensor", "ST")

        if KDBG and th == 1:
            p.wait("gpsimd", "VV")
            p.wait("gpsimd", "SS")
            p.dma("gpsimd", dbg_big[:, 0:1024], g16[:, 62 * THALF:64 * THALF], "ST")
            p.wait("gpsimd", "ST")
        p.barrier("tensor", "vector", "scalar")
        p.op("vector", lambda E: E.memset(mx0[:, :], 0.0), "VV")
        sq_mm = [0, 0]
        tree_vv = [0, 0]
        for f in range(FTL):
            sq = (scb0, scb1)[f % 2]
            p.wait_at("scalar", "TN", sq_mm[f % 2])
            p.wait_at("scalar", "VV", tree_vv[f % 2])
            svv = p.op("scalar", lambda E, f=f, sq=sq: E.activation(
                sq[:, 0:THALF], g16[:, f * THALF:(f + 1) * THALF], AF.Square, bias=0.0, scale=1.0), "SS")
            p.wait_at("tensor", "SS", svv)
            sq_mm[f % 2] = p.op("tensor", lambda E, f=f, sq=sq: E.matmul(
                pS[0:1, 0:THALF], onescb[:, 0:1], sq[:, 0:THALF],
                start=(f == 0), stop=(f == FTL - 1)), "TN")
            src_m = (mx0, mx1)[f % 2]
            dst_m = (mx1, mx0)[f % 2]
            p.wait_at("vector", "SS", svv)
            tree_vv[f % 2] = p.op("vector", lambda E, f=f, src_m=src_m, dst_m=dst_m, sq=sq: E.tensor_tensor(
                dst_m[:, :], src_m[:, :], sq[:, 0:THALF], ALU.max), "VV")
        mxf = (mx1, mx0)[(FTL - 1) % 2]
        p.wait("tensor", "VV")
        for t in range(NTT):
            pt = (pT0, pT1)[t % 2]
            p.wait_at("tensor", "VV", tr_state["evac"][t % 2])
            tv = p.op("tensor", lambda E, t=t, pt=pt, mxf=mxf: E.transpose(
                pt[:, :], mxf[:, t * 128:(t + 1) * 128], idtb[:, :]), "TN")
            p.wait_at("vector", "TN", tv)
            p.op("vector", lambda E, pt=pt: E.tensor_copy(scb0[:, 0:128], pt[:, :]))
            vsp()
            p.op("vector", lambda E, t=t, th=th: E.reduce_max(
                stats[:, 64 + th * NTT + t:65 + th * NTT + t], scb0[:, 0:128], axis=AX))
            ev = vspi()
            tr_state["evac"][t % 2] = ev
        p.wait("vector", "TN")
        p.op("vector", lambda E: E.tensor_copy(rowA[0:1, 0:THALF], pS[0:1, 0:THALF]), "VV")
        row_to_col(rowA[0:1, 0:THALF], stats[:, 72:72 + NTT], NTT)
        p.wait("scalar", "VV")
        p.op("scalar", lambda E: E.activation(scf0[:, 0:NTT], stats[:, 72:72 + NTT], AF.Ln, bias=cst[:, 1:2], scale=cst[:, 6:7]))
        ssp()
        p.op("scalar", lambda E: E.activation(scf0[:, 0:NTT], scf0[:, 0:NTT], AF.Exp, bias=0.0, scale=cst[:, 3:4]))
        ssp()
        p.op("scalar", lambda E, th=th: E.activation(scf0[:, NTT:2 * NTT], stats[:, 64 + th * NTT:64 + (th + 1) * NTT], AF.Ln, bias=0.0, scale=1.0))
        ssp()
        p.op("scalar", lambda E: E.activation(scf0[:, NTT:2 * NTT], scf0[:, NTT:2 * NTT], AF.Exp, bias=0.0, scale=cst[:, 7:8]))
        svv = sspi()
        p.wait_at("vector", "SS", svv)
        p.op("vector", lambda E: E.tensor_tensor(scf1[:, 0:NTT], scf0[:, NTT:2 * NTT], scf0[:, 0:NTT], ALU.mult))
        vsp()
        p.op("vector", lambda E: E.tensor_scalar(scf1[:, 0:NTT], scf1[:, 0:NTT], Q_EPS, None, ALU.max))
        vsp()
        p.op("vector", lambda E: E.reciprocal(scf1[:, NTT:2 * NTT], scf1[:, 0:NTT]))
        vsp()
        p.op("vector", lambda E: E.tensor_scalar(scf1[:, 0:NTT], scf1[:, NTT:2 * NTT], 127.0, None, ALU.mult))
        vsp()
        p.op("vector", lambda E: E.tensor_tensor(qscol[:, 0:NTT], scf0[:, 0:NTT], scf1[:, 0:NTT], ALU.mult))
        vsp()
        p.op("vector", lambda E: E.reciprocal(scf1[:, NTT:2 * NTT], scf1[:, 0:NTT]))
        vsp()
        p.op("vector", lambda E, th=th: E.tensor_scalar(
            escT[:, th * NTT:(th + 1) * NTT], scf1[:, NTT:2 * NTT], thrS[:, 6:7], None, ALU.mult))
        vv = vspi()
        vsp()
        cr = col_to_row(qscol[:, 0:NTT], rowA[0:1, 0:THALF], NTT)
        bcast_half(onesrow, rowA[0:1, 0:THALF], qscb[:, :], 0, n=THALF)

        p.wait("vector", "VV")
        for f in range(FTL):
            for hh in range(2):
                sl = slice(f * THALF + hh * 256, f * THALF + (hh + 1) * 256)
                p.op("vector", lambda E, sl=sl, hh=hh: E.tensor_tensor(
                    scf0[:, 0:256], g16[:, sl], qscb[:, hh * 256:(hh + 1) * 256], ALU.mult))
                p.op("vector", lambda E: E.tensor_copy(ibuf[:, 0:256], scf0[:, 0:256]))
                p.op("vector", lambda E, sl=sl: E.tensor_copy(g16[:, sl], ibuf[:, 0:256]), "VV")

        p.barrier("tensor", "vector", "scalar", "sync")
        dn_evac = 0
        for ds in range(4):
            slab_guard = [0, 0]
            p.wait_at("tensor", "VV", dn_evac)
            for f in range(FTL):
                tdT = (tdT0, tdT1)[f % 2]
                lts = ("LT0", "LT1")[f % 2]
                p.wait_at("sync", "TN", slab_guard[f % 2])
                lv = p.dmaT("sync", tdT[:, :], tern["d"][ds * 512:(ds + 1) * 512, f * 128:(f + 1) * 128], lts)
                p.wait_at("tensor", lts, lv)
                for t in range(NTT):
                    slab_guard[f % 2] = p.op("tensor", lambda E, f=f, t=t, tdT=tdT: E.matmul(
                        pD[t][:, :], g16[:, f * THALF + t * 128: f * THALF + (t + 1) * 128], tdT[:, :],
                        start=(f == 0), stop=(f == FTL - 1)), "TN")
            p.wait("vector", "TN")
            for t in range(NTT):
                tg_ = th * NTT + t
                xr = (xr0, xr1)[t % 2]
                yr = (yr0, yr1)[t % 2]
                ob = (ob0, ob1)[t % 2]
                lrs = ("LR0", "LR1")[t % 2]
                sos = ("SO0", "SO1")[t % 2]
                p.wait_at("sync", "VV", rl_used[t % 2])
                lx = p.dma("sync", xr[:, :], x_in[tg_ * 128:(tg_ + 1) * 128, ds * 512:(ds + 1) * 512], lrs)
                ly = p.dma("sync", yr[:, :], yat[tg_ * 128:(tg_ + 1) * 128, ds * 512:(ds + 1) * 512], lrs)
                p.wait_at("vector", lrs, ly)
                p.wait_at("vector", sos, out_store[t % 2])
                p.op("vector", lambda E, t=t, tg_=tg_, xr=xr, ob=ob: E.scalar_tensor_tensor(
                    ob[:, :], pD[t][:, :], escT[:, tg_:tg_ + 1], xr[:, :], ALU.mult, ALU.add))
                rl_used[t % 2] = p.op("vector", lambda E, ob=ob, yr=yr: E.tensor_tensor(
                    ob[:, :], ob[:, :], yr[:, :], ALU.add), "VV")
                p.wait_at("gpsimd", "VV", rl_used[t % 2])
                out_store[t % 2] = p.dma("gpsimd", out_hbm[tg_ * 128:(tg_ + 1) * 128, ds * 512:(ds + 1) * 512],
                                         ob[:, :], sos)
            dn_evac = p.val("VV")
    if KDBG:
        p.wait("gpsimd", "VV")
        p.wait("gpsimd", "SS")
        p.wait("gpsimd", "TN")
        for dsts, srcs in ((dbg_small[:, 0:8], escT[:, :]), (dbg_small[:, 8:16], qscol[:, :]),
                           (dbg_small[:, 16:24], srcol[:, :]), (dbg_small[:, 24:32], thrS[:, :]),
                           (dbg_small[:, 32:112], stats[:, :]), (dbg_small[:, 112:120], cst[:, :]),
                           (dbg_small[:, 120:121], sgcb[:, 0:1]), (dbg_small[:, 121:122], qscb[:, 0:1]),
                           (dbg_small[0:1, 122:130], swS[0:1, :])):
            p.dma("gpsimd", dsts, srcs, "ST", allow_slow_non_contiguous=True)
        p.dma("gpsimd", dbg_yat[:, :], yat[:, :], "ST")
        p.dma("gpsimd", dbg_g16[:, :], g16[:, :], "ST")
        p.wait("gpsimd", "ST")
    p.wait("gpsimd", "SO0")
    p.wait("gpsimd", "SO1")
    p.wait("sync", "SO0")
    p.wait("sync", "SO1")

    return _assemble(nc, p, stack)


def _assemble(nc, p, stack):
    with nc.Block() as block:
        @block.sync
        def _(E):
            for f in p.q["sync"]:
                f(E)

        @block.vector
        def _(E):
            for f in p.q["vector"]:
                f(E)

        @block.scalar
        def _(E):
            for f in p.q["scalar"]:
                f(E)

        @block.tensor
        def _(E):
            for f in p.q["tensor"]:
                f(E)

        @block.gpsimd
        def _(E):
            for f in p.q["gpsimd"]:
                f(E)

    return nc


_NC_CACHE = [None]


def kernel(**inputs):
    x = np.ascontiguousarray(np.asarray(inputs["L0_attn_x"], dtype=np.float32).reshape(8192, 2048))
    alog = np.ascontiguousarray(np.asarray(inputs["L0_attn_A_log"], dtype=np.float32))
    dv = np.ascontiguousarray(np.asarray(inputs["L0_attn_D"], dtype=np.float32).reshape(D, 1))
    wxp = np.ascontiguousarray(np.asarray(inputs["L0_attn_W_xproj"], dtype=np.float32))
    wg = np.ascontiguousarray(np.asarray(inputs["L0_ffn_w_gate"], dtype=np.float32))
    wu = np.ascontiguousarray(np.asarray(inputs["L0_ffn_w_up"], dtype=np.float32))
    wd = np.ascontiguousarray(np.asarray(inputs["L0_ffn_w_down"], dtype=np.float32))
    idb = np.eye(128, dtype=ml_dtypes.bfloat16)
    idf = np.eye(128, dtype=np.float32)
    selm = np.zeros((32, 32 * 128), dtype=ml_dtypes.bfloat16)
    for i in range(32):
        selm[i, i * 128:(i + 1) * 128] = 1.0

    if _NC_CACHE[0] is None:
        _NC_CACHE[0] = build()
    nc = _NC_CACHE[0]

    in_maps = []
    for c in range(NCORES):
        cm = np.zeros((128, NCORES), dtype=np.float32)
        cm[:, :c] = 1.0
        in_maps.append({
            "x_tok": x[c * TOK:(c + 1) * TOK],
            "wxp": wxp, "alog": alog, "dvec": dv,
            "wg": wg, "wu": wu, "wd": wd,
            "idtb": idb, "idtf": idf, "selm": selm,
            "cm": cm, "cm1": np.ascontiguousarray(1.0 - cm),
        })
    res = run_bass_kernel_spmd(nc, in_maps, core_ids=list(range(NCORES)))
    out = np.concatenate([res.results[c]["out"] for c in range(NCORES)], axis=0)
    return out.reshape(1, 8192, 2048)
